# version 09
# baseline (speedup 1.0000x reference)
import numpy as np
import ml_dtypes
import concourse.bass as bass
import concourse.mybir as mybir
from concourse.alu_op_type import AluOpType
from concourse.bass_utils import run_bass_kernel_spmd
from contextlib import ExitStack

F32 = mybir.dt.float32
BF16 = mybir.dt.bfloat16
AF = mybir.ActivationFunctionType
AX = mybir.AxisListType.X

D, DC, H, DK = 2048, 16, 16, 128
T, NCH, SLABS, T_SLAB, CPS = 2048, 16, 4, 512, 4
EPS = 1e-5
SCALE = DK ** -0.5
RG = [[0, 1, 2, 3], [4, 5, 6, 7]]
NE, I_FULL, IB, I_BLK, ICB = 32, 768, 3, 256, 2
T_HALF, TSF, TT, K_TOP = 1024, 2, 8, 8

DATAPATH_OPS = {
    'tensor_tensor', 'tensor_scalar', 'tensor_single_scalar', 'tensor_copy',
    'tensor_scalar_mul', 'tensor_scalar_add', 'tensor_add', 'tensor_mul', 'tensor_sub',
    'tensor_reduce', 'reduce_max', 'reduce_sum', 'reciprocal', 'memset', 'select',
    'scalar_tensor_tensor', 'activation', 'copy', 'mul', 'add', 'sqrt', 'square', 'iota',
    'tensor_tensor_scan', 'bn_stats', 'bn_aggr', 'transpose',
}


class SerialEng:
    def __init__(self, eng):
        self._e = eng

    def __getattr__(self, name):
        f = getattr(self._e, name)
        if name in DATAPATH_OPS:
            def wrapped(*a, **k):
                r = f(*a, **k)
                self._e.drain()
                return r
            return wrapped
        return f


class Sem:
    def __init__(self, es, nc, name):
        self.h = es(nc.semaphore(name))
        self.n = 0

    def inc(self, instr, k=1):
        instr.then_inc(self.h, k)
        self.n += k
        return self.n


def build():
    nc = bass.Bass("TRN2", target_bir_lowering=False)
    ap = lambda name, shape, dt: nc.dram_tensor(name, shape, dt, kind="ExternalInput").ap()
    x_in = ap("x", [T, D], F32)
    wqT_in = ap("wqT", [D, D], F32)
    wkT_in = ap("wkT", [D, D], F32)
    wvT_in = ap("wvT", [D, D], F32)
    woT_in = ap("woT", [D, D], F32)
    fm_ins = {n: ap(n, [128, 128], F32) for n in ("fq1T", "fq2T", "fk1T", "fk2T")}
    b_ins = {n: ap(n, [128, 1], F32) for n in ("fq1b", "fq2b", "fk1b", "fk2b")}
    nw_bc_in = ap("nw_bc", [128, 128], F32)
    mask_in = ap("maskf", [128, 128], F32)
    idb_in = nc.dram_tensor("id_bf", [128, 128], BF16, kind="ExternalInput").ap()
    idf_in = ap("id_f32", [128, 128], F32)
    pm_in = ap("pm", [128, 4], F32)
    routerT_in = ap("routerT", [D, 32], F32)
    wg_in = ap("wg", [NE, D, I_FULL], F32)
    wu_in = ap("wu", [NE, D, I_FULL], F32)
    wd_in = ap("wd", [NE, I_FULL, D], F32)

    out_dram = nc.dram_tensor("out", [T, D], F32, kind="ExternalOutput").ap()

    xn_scr = nc.dram_tensor("xn_scr", [T, D], BF16).ap()
    kf_scr = nc.dram_tensor("kf_scr", [D, T], BF16).ap()
    vtm_scr = nc.dram_tensor("vtm_scr", [H, T, DK], BF16).ap()
    kv_scr = nc.dram_tensor("kv_scr", [H, NCH, 128, DK], BF16).ap()
    A_scr = nc.dram_tensor("A_scr", [D, T], F32).ap()
    w_scr = nc.dram_tensor("w_scr", [32, T], F32).ap()
    cc_in_h = nc.dram_tensor("cc_in", [H * 128, DK], F32)
    cc_out_h = nc.dram_tensor("cc_out", [4 * H * 128, DK], F32)
    cc_in, cc_out = cc_in_h.ap(), cc_out_h.ap()

    stack = ExitStack()
    es = stack.enter_context

    s_x = [Sem(es, nc, f"s_x{i}") for i in range(2)]
    s_boot = Sem(es, nc, "s_boot")
    s_bootg = Sem(es, nc, "s_bootg")
    s_xnst = Sem(es, nc, "s_xnst")
    s_xnfm = Sem(es, nc, "s_xnfm")
    s_wk = [Sem(es, nc, f"s_wk{i}") for i in range(2)]
    s_wv = [Sem(es, nc, f"s_wv{i}") for i in range(2)]
    s_wq = [Sem(es, nc, f"s_wq{i}") for i in range(2)]
    s_wo = [Sem(es, nc, f"s_wo{i}") for i in range(2)]
    s_kf2 = [Sem(es, nc, f"s_kf{i}") for i in range(2)]
    s_vt2 = [Sem(es, nc, f"s_vt{i}") for i in range(2)]
    s_kvn = [Sem(es, nc, f"s_kvn{i}") for i in range(2)]
    s_cci = Sem(es, nc, "s_cci")
    s_A = [Sem(es, nc, f"s_A{i}") for i in range(2)]
    s_cc = Sem(es, nc, "s_cc")
    s_kvr = Sem(es, nc, "s_kvr")
    s_kvl = [Sem(es, nc, f"s_kvl{i}") for i in range(2)]
    s_sl = Sem(es, nc, "s_sl")
    s_gu = [Sem(es, nc, f"s_gu{i}") for i in range(2)]
    s_wd = Sem(es, nc, "s_wd")
    s_wsc = Sem(es, nc, "s_wsc")
    s_wr = [Sem(es, nc, f"s_wr{i}") for i in range(2)]
    s_xnf = Sem(es, nc, "s_xnf")
    s_Al = [Sem(es, nc, f"s_Al{i}") for i in range(2)]
    s_xo = [Sem(es, nc, f"s_xo{i}") for i in range(2)]
    s_out = [Sem(es, nc, f"s_out{i}") for i in range(2)]
    s_pe = Sem(es, nc, "s_pe")
    s_sc = Sem(es, nc, "s_sc")
    s_ev = Sem(es, nc, "s_ev")
    s_gv = Sem(es, nc, "s_gv")

    idf = es(nc.sbuf_tensor("idf_s", [128, 128], F32))
    rT = es(nc.sbuf_tensor("rT_s", [128, DC, 32], BF16))
    ones_row = es(nc.sbuf_tensor("ones_row", [1, 128], F32))

    block = es(nc.Block())
    t_ = nc.tensor
    v_ = SerialEng(nc.vector)
    c_ = SerialEng(nc.scalar)
    g_ = nc.gpsimd
    y_ = nc.sync

    with ExitStack() as aph:
        aes = aph.enter_context
        xn_fm = aes(nc.sbuf_tensor("xn_fm", [128, DC, T], BF16))
        KVf = aes(nc.sbuf_tensor("KVf", [128, H, DK], F32))
        KVb = aes(nc.sbuf_tensor("KVb", [128, H, DK], BF16))
        fmT = {n: aes(nc.sbuf_tensor(n + "_s", [128, 128], BF16)) for n in ("fq1T", "fq2T", "fk1T", "fk2T")}
        fmB = {n: aes(nc.sbuf_tensor(n + "_s", [128, 1], F32)) for n in ("fq1b", "fq2b", "fk1b", "fk2b")}
        kvrank = aes(nc.sbuf_tensor("kvrank", [128, H, DK], F32))
        fq1bs = aes(nc.sbuf_tensor("fq1bs", [128, 1], F32))
        nw_bc = aes(nc.sbuf_tensor("nw_bc_s", [128, 128], F32))
        maskf = aes(nc.sbuf_tensor("maskf_s", [128, 128], F32))
        idb = aes(nc.sbuf_tensor("idb_s", [128, 128], BF16))
        pm = aes(nc.sbuf_tensor("pm_s", [128, 4], F32))

        for dst, srcp in [(nw_bc, nw_bc_in), (maskf, mask_in), (pm, pm_in), (idb, idb_in),
                          (idf, idf_in)] + [(fmB[n], b_ins[n]) for n in fmB]:
            s_boot.inc(y_.dma_start(out=dst[:], in_=srcp), 16)
        for n in fmT:
            s_bootg.inc(g_.dma_start(out=fmT[n][:], in_=fm_ins[n]), 16)
        s_bootg.inc(g_.dma_start(out=rT[:], in_=routerT_in.rearrange("(dc p) e -> p dc e", p=128)), 16)
        BOOT, BOOTG = s_boot.n, s_bootg.n
        v_.wait_ge(s_boot.h, BOOT)
        tok_fq1bs = s_ev.inc(v_.tensor_scalar_mul(fq1bs[:], fmB["fq1b"][:], SCALE))
        tok_kv0 = s_ev.inc(v_.memset(KVf[:], 0.0))
        tok_ones = s_ev.inc(v_.memset(ones_row[:], 1.0))

        with ExitStack() as ph:
            pes = ph.enter_context
            xt = [pes(nc.sbuf_tensor(f"xt{i}", [128, D], F32)) for i in range(2)]
            xc = [pes(nc.sbuf_tensor(f"xc{i}", [128, D], F32)) for i in range(2)]
            sqt = pes(nc.sbuf_tensor("sqt", [128, D], F32))
            xnt = [pes(nc.sbuf_tensor(f"xnt{i}", [128, D], BF16)) for i in range(2)]
            mcol = pes(nc.sbuf_tensor("mcol", [128, 1], F32))
            vcol = pes(nc.sbuf_tensor("vcol", [128, 1], F32))
            sdcol = pes(nc.sbuf_tensor("sdcol", [128, 1], F32))
            rstd = pes(nc.sbuf_tensor("rstd", [128, 1], F32))

            tok_x = {0: s_x[0].inc(y_.dma_start(out=xt[0][:], in_=x_in[0:128, :]), 16),
                     1: s_x[1].inc(y_.dma_start(out=xt[1][:], in_=x_in[128:256, :]), 16)}
            tok_xnst = {}
            for tt in range(NCH):
                par = tt % 2
                v_.wait_ge(s_x[par].h, tok_x[tt])
                v_.reduce_sum(mcol[:], xt[par][:], axis=AX)
                v_.tensor_scalar_mul(mcol[:], mcol[:], 1.0 / D)
                tok_xc = s_ev.inc(v_.tensor_scalar(xc[par][:], xt[par][:], mcol[:], None, AluOpType.subtract))
                c_.wait_ge(s_ev.h, tok_xc)
                tok_sq = s_sc.inc(c_.square(sqt[:], xc[par][:]))
                v_.wait_ge(s_sc.h, tok_sq)
                v_.reduce_sum(vcol[:], sqt[:], axis=AX)
                tok_var = s_ev.inc(v_.tensor_scalar(vcol[:], vcol[:], 1.0 / D, EPS,
                                                    AluOpType.mult, AluOpType.add))
                c_.wait_ge(s_ev.h, tok_var)
                tok_sd = s_sc.inc(c_.sqrt(sdcol[:], vcol[:]))
                v_.wait_ge(s_sc.h, tok_sd)
                v_.reciprocal(rstd[:], sdcol[:])
                tok_xn = s_ev.inc(v_.tensor_scalar(xnt[par][:], xc[par][:], rstd[:], None, AluOpType.mult))
                y_.wait_ge(s_ev.h, tok_xn)
                tok_xnst[tt] = s_xnst.inc(
                    y_.dma_start(out=xn_scr[tt * 128:(tt + 1) * 128, :], in_=xnt[par][:]), 16)
                if tt + 2 < NCH:
                    y_.wait_ge(s_xnst.h, tok_xnst[tt])
                    tok_x[tt + 2] = s_x[par].inc(
                        y_.dma_start(out=xt[par][:], in_=x_in[(tt + 2) * 128:(tt + 3) * 128, :]), 16)
            y_.wait_ge(s_xnst.h, s_xnst.n)
            for dc in range(DC):
                s_xnfm.inc(y_.dma_start(out=xn_fm[:, dc, :], in_=xn_scr[:, dc * 128:(dc + 1) * 128],
                                        transpose=True), 16)
            XNFM = s_xnfm.n

        with ExitStack() as ph:
            pes = ph.enter_context
            wkb = [pes(nc.sbuf_tensor(f"wkb{i}", [128, DC, 128], BF16)) for i in range(2)]
            wvb = [pes(nc.sbuf_tensor(f"wvb{i}", [128, DC, 128], BF16)) for i in range(2)]
            kraw_ = [pes(nc.sbuf_tensor(f"kraw{i}", [128, T], BF16)) for i in range(2)]
            vraw_ = [pes(nc.sbuf_tensor(f"vraw{i}", [128, T], BF16)) for i in range(2)]
            s1b = pes(nc.sbuf_tensor("s1b", [128, T_SLAB], F32))
            s2b = pes(nc.sbuf_tensor("s2b", [128, T_SLAB], F32))
            kfb_ = [pes(nc.sbuf_tensor(f"kfb{i}", [128, T], BF16)) for i in range(2)]
            ktm_ = [pes(nc.sbuf_tensor(f"ktm{i}", [128, NCH, 128], BF16)) for i in range(2)]
            vtm_ = [pes(nc.sbuf_tensor(f"vtm{i}", [128, NCH, 128], BF16)) for i in range(2)]
            kvn_h = pes(nc.sbuf_tensor("kvn_h", [128, NCH, 128], BF16))
            kvt_s = pes(nc.sbuf_tensor("kvt_s", [128, H, DK], F32))
            ps_a = [pes(nc.psum_tensor(f"a1_psa{i}", [128, T_SLAB], F32)) for i in range(2)]
            ps_t1 = pes(nc.psum_tensor("a1_pst1", [128, 1024], BF16))
            ps_t2 = pes(nc.psum_tensor("a1_pst2", [128, 1024], BF16))
            ps_kv = [pes(nc.psum_tensor(f"a1_pskv{i}", [128, 512], F32)) for i in range(2)]
            ps_kvt = pes(nc.psum_tensor("a1_pskvt", [128, 512], F32))

            def load_w(w_in, dst, sem, h):
                src = w_in[:, h * 128:(h + 1) * 128].rearrange("(dc p) m -> p dc m", p=128)
                return sem.inc(g_.dma_start(out=dst[:], in_=src), 16)

            g_.wait_ge(s_xnst.h, s_xnst.n)
            tok_wk = {h: load_w(wkT_in, wkb[h], s_wk[h], h) for h in range(2)}
            tok_wv = {h: load_w(wvT_in, wvb[h], s_wv[h], h) for h in range(2)}

            t_.wait_ge(s_xnfm.h, XNFM)
            t_.wait_ge(s_bootg.h, BOOTG)
            t_.wait_ge(s_boot.h, BOOT)
            tok_kfst = {}
            tok_vtmst = {}
            tok_kvst = {}
            tok_prjev = {}
            tok_kraw = {}
            prj = 0

            def emit_proj(h):
                par = h % 2
                t_.wait_ge(s_wk[par].h, tok_wk[h])
                t_.wait_ge(s_wv[par].h, tok_wv[h])
                for wb, raw in [(wkb[par], kraw_[par]), (wvb[par], vraw_[par])]:
                    for sl in range(SLABS):
                        p2 = pstate["prj"] % 2
                        if pstate["prj"] >= 2:
                            t_.wait_ge(s_sc.h, tok_prjev[pstate["prj"] - 2])
                        for dc in range(DC):
                            mm = t_.matmul(ps_a[p2][:], wb[:, dc, :],
                                           xn_fm[:, dc, sl * T_SLAB:(sl + 1) * T_SLAB],
                                           start=(dc == 0), stop=(dc == DC - 1))
                        tok = s_pe.inc(mm)
                        c_.wait_ge(s_pe.h, tok)
                        tok_prjev[pstate["prj"]] = s_sc.inc(
                            c_.copy(raw[:, sl * T_SLAB:(sl + 1) * T_SLAB], ps_a[p2][:]))
                        pstate["prj"] += 1
                tok_kraw[h] = tok_prjev[pstate["prj"] - 1]
                if h + 2 < H:
                    g_.wait_ge(s_sc.h, tok_kraw[h])
                    tok_wk[h + 2] = load_w(wkT_in, wkb[par], s_wk[par], h + 2)
                    tok_wv[h + 2] = load_w(wvT_in, wvb[par], s_wv[par], h + 2)

            def emit_rest(h):
                par = h % 2
                kfb = kfb_[par]
                vtm = vtm_[par]
                ktm = ktm_[par]
                kraw = kraw_[par]
                vraw = vraw_[par]
                if h > 1:
                    v_.wait_ge(s_kf2[par].h, tok_kfst[h - 2])
                    v_.wait_ge(s_vt2[par].h, tok_vtmst[h - 2])
                t_.wait_ge(s_sc.h, tok_kraw[h])
                for sl in range(SLABS):
                    tok1 = s_pe.inc(t_.matmul(ps_kv[0][:], fmT["fk1T"][:],
                                              kraw[:, sl * T_SLAB:(sl + 1) * T_SLAB], start=True, stop=True))
                    tok2 = s_pe.inc(t_.matmul(ps_kv[1][:], fmT["fk2T"][:],
                                              kraw[:, sl * T_SLAB:(sl + 1) * T_SLAB], start=True, stop=True))
                    c_.wait_ge(s_pe.h, tok1)
                    tok_s1 = s_sc.inc(c_.activation(s1b[:], ps_kv[0][:], AF.Identity,
                                                    bias=fmB["fk1b"][:], scale=1.0))
                    v_.wait_ge(s_sc.h, tok_s1)
                    v_.wait_ge(s_pe.h, tok2)
                    tok_kf = s_ev.inc(v_.scalar_tensor_tensor(kfb[:, sl * T_SLAB:(sl + 1) * T_SLAB],
                                                              ps_kv[1][:], fmB["fk2b"][:], s1b[:],
                                                              AluOpType.add, AluOpType.mult))
                    t_.wait_ge(s_ev.h, tok_kf)
                y_.wait_ge(s_ev.h, s_ev.n)
                tok_kfst[h] = s_kf2[par].inc(
                    y_.dma_start(out=kf_scr[h * 128:(h + 1) * 128, :], in_=kfb[:]), 16)
                trstep = 0
                tok_trev = {}
                for c in range(NCH):
                    for srcbuf, dstbuf in ((kfb, ktm), (vraw, vtm)):
                        bank = ps_t1 if trstep % 2 == 0 else ps_t2
                        if trstep >= 2:
                            t_.wait_ge(s_ev.h, tok_trev[trstep - 2])
                        tok_tr = s_pe.inc(t_.transpose(bank[:, 0:128],
                                                       srcbuf[:, c * 128:(c + 1) * 128], idb[:]))
                        v_.wait_ge(s_pe.h, tok_tr)
                        tok_trev[trstep] = s_ev.inc(v_.tensor_copy(dstbuf[:, c, :], bank[:, 0:128]))
                        trstep += 1
                t_.wait_ge(s_ev.h, tok_trev[trstep - 1])
                t_.wait_ge(s_ev.h, tok_trev[trstep - 2])
                y_.wait_ge(s_ev.h, s_ev.n)
                tok_vtmst[h] = s_vt2[par].inc(
                    y_.dma_start(out=vtm_scr[h].rearrange("(c p) e -> p c e", p=128), in_=vtm[:]), 16)
                if h > 0:
                    c_.wait_ge(s_kvn[0].h, tok_kvst.get(0, 0))
                tok_kvev = {}
                for c in range(NCH):
                    pp = c % 2
                    if c >= 2:
                        t_.wait_ge(s_sc.h, tok_kvev[c - 2])
                    tokn = s_pe.inc(t_.matmul(ps_kv[pp][:, 0:128], ktm[:, c, :], vtm[:, c, :],
                                              start=True, stop=True))
                    s_pe.inc(t_.matmul(ps_kvt[:, 0:128], ktm[:, c, :], vtm[:, c, :],
                                       start=(c == 0), stop=(c == NCH - 1)))
                    c_.wait_ge(s_pe.h, tokn)
                    tok_kvev[c] = s_sc.inc(c_.copy(kvn_h[:, c, :], ps_kv[pp][:, 0:128]))
                y_.wait_ge(s_sc.h, tok_kvev[NCH - 1])
                tok_kvst[0] = s_kvn[0].inc(y_.dma_start(
                    out=kv_scr[h].rearrange("c p e -> p c e"), in_=kvn_h[:]), 16)
                c_.wait_ge(s_pe.h, s_pe.n)
                tok_kvt = s_sc.inc(c_.copy(kvt_s[:, h, :], ps_kvt[:, 0:128]))
                t_.wait_ge(s_sc.h, tok_kvt)

            pstate = {"prj": 0}
            for h in range(H + 1):
                if h < H:
                    emit_proj(h)
                if h >= 1:
                    emit_rest(h - 1)
            y_.wait_ge(s_sc.h, s_sc.n)
            tok_ccst = s_cci.inc(y_.dma_start(out=cc_in.rearrange("(h p) e -> p h e", p=128),
                                              in_=kvt_s[:]), 16)
            g_.wait_ge(s_cci.h, tok_ccst)
            s_cc.inc(g_.collective_compute("AllGather", mybir.AluOpType.bypass, replica_groups=RG,
                                           ins=[cc_in_h.ap().opt()], outs=[cc_out_h.ap().opt()]))
            A1_SC_DONE = s_sc.n

        with ExitStack() as ph:
            pes = ph.enter_context
            wqb = [pes(nc.sbuf_tensor(f"wqb{i}", [128, DC, 128], BF16)) for i in range(2)]
            wob = [pes(nc.sbuf_tensor(f"wob{i}", [128, H, 128], BF16)) for i in range(2)]
            qraw = pes(nc.sbuf_tensor("qraw", [128, T_SLAB], BF16))
            s1b = pes(nc.sbuf_tensor("a2s1", [128, T_SLAB], F32))
            qf = pes(nc.sbuf_tensor("qf", [128, H, T_SLAB], BF16))
            kf_sl = pes(nc.sbuf_tensor("kf_sl", [128, H, T_SLAB], BF16))
            vtm_sl = pes(nc.sbuf_tensor("vtm_sl", [128, H, CPS, 128], BF16))
            kvn = [pes(nc.sbuf_tensor(f"kvn{i}", [128, H, DK], BF16)) for i in range(3)]
            sbf_all = pes(nc.sbuf_tensor("sbf_all", [128, H, 128], BF16))
            o1_all = pes(nc.sbuf_tensor("o1_all", [128, H, 128], F32))
            o2_all = pes(nc.sbuf_tensor("o2_all", [128, H, 128], BF16))
            ssc16 = pes(nc.sbuf_tensor("ssc16", [128, 16], F32))
            sd16 = pes(nc.sbuf_tensor("sd16", [128, 16], F32))
            rst16 = pes(nc.sbuf_tensor("rst16", [128, 16], F32))
            o_sl = pes(nc.sbuf_tensor("o_sl", [128, CPS, H, 128], BF16))
            A_sl = [pes(nc.sbuf_tensor("A_sl0", [128, T_SLAB], F32))] * 2
            FB = [pes(nc.psum_tensor(f"a2_f{i}", [128, 512], F32)) for i in range(6)]
            TRB = [pes(nc.psum_tensor(f"a2_tr{i}", [128, 1024], BF16)) for i in range(2)]

            def load_wq(h, slot):
                src = wqT_in[:, h * 128:(h + 1) * 128].rearrange("(dc p) m -> p dc m", p=128)
                return s_wq[slot].inc(g_.dma_start(out=wqb[slot][:], in_=src), 16)

            def load_wo(gi, slot):
                m = gi % DC
                src = woT_in[:, m * 128:(m + 1) * 128].rearrange("(hh p) m -> p hh m", p=128)
                return s_wo[slot].inc(g_.dma_start(out=wob[slot][:], in_=src), 16)

            g_.wait_ge(s_sc.h, A1_SC_DONE)
            y_.wait_ge(s_sc.h, A1_SC_DONE)
            g_.wait_ge(s_cci.h, 16)
            y_.wait_ge(s_cci.h, 16)
            for i in range(2):
                y_.wait_ge(s_kf2[i].h, s_kf2[i].n)
                y_.wait_ge(s_vt2[i].h, s_vt2[i].n)
                g_.wait_ge(s_kf2[i].h, s_kf2[i].n)
                g_.wait_ge(s_vt2[i].h, s_vt2[i].n)
            y_.wait_ge(s_kvn[0].h, s_kvn[0].n)
            tok_wq = {0: load_wq(0, 0), 1: load_wq(1, 1)}
            tok_wo = {0: load_wo(0, 0), 1: load_wo(1, 1)}
            tok_kvl = {}
            tok_kvbcvt = {}
            tok_sTb = {}
            tok_sbfb = {}
            tok_ob = {}
            tok_o1ev = {}
            tok_o2 = {}
            tok_trb = {}
            tok_oslev = {}
            tok_Ae = {}
            tok_Ast = {}
            tok_qf = 0

            tok_baseupd = {-1: 0}
            for sl in range(SLABS):
                if sl > 0:
                    y_.wait_ge(s_pe.h, s_pe.n)
                    v_.wait_ge(s_pe.h, s_pe.n)
                    t_.wait_ge(s_ev.h, s_ev.n)
                    t_.wait_ge(s_sc.h, s_sc.n)
                for h in range(H):
                    s_sl.inc(y_.dma_start(
                        out=kf_sl[:, h, :],
                        in_=kf_scr[h * 128:(h + 1) * 128, sl * T_SLAB:(sl + 1) * T_SLAB]), 16)
                    s_sl.inc(y_.dma_start(
                        out=vtm_sl[:, h, :, :],
                        in_=vtm_scr[h, sl * T_SLAB:(sl + 1) * T_SLAB, :].rearrange("(c p) e -> p c e", p=128)),
                        16)
                if sl > 0:
                    y_.wait_ge(s_gv.h, tok_baseupd[sl - 1])
                for ci in range(CPS - 1):
                    s_sl.inc(y_.dma_start(
                        out=kvn[ci][:], in_=kv_scr[:, sl * CPS + ci, :, :].rearrange("h p e -> p h e")), 16)
                SLAB_LOADS = s_sl.n
                for h in range(H):
                    par = h % 2
                    t_.wait_ge(s_wq[par].h, tok_wq[sl * H + h])
                    for dc in range(DC):
                        mm = t_.matmul(FB[0][:], wqb[par][:, dc, :],
                                       xn_fm[:, dc, sl * T_SLAB:(sl + 1) * T_SLAB],
                                       start=(dc == 0), stop=(dc == DC - 1))
                    tok_q = s_pe.inc(mm)
                    c_.wait_ge(s_pe.h, tok_q)
                    tok_qr = s_sc.inc(c_.copy(qraw[:], FB[0][:]))
                    t_.wait_ge(s_sc.h, tok_qr)
                    gq = sl * H + h + 2
                    if gq < SLABS * H:
                        g_.wait_ge(s_sc.h, tok_qr)
                        tok_wq[gq] = load_wq(gq % H, par)
                    tok1 = s_pe.inc(t_.matmul(FB[1][:], fmT["fq1T"][:], qraw[:], start=True, stop=True))
                    tok2 = s_pe.inc(t_.matmul(FB[2][:], fmT["fq2T"][:], qraw[:], start=True, stop=True))
                    c_.wait_ge(s_pe.h, tok1)
                    if sl * H + h > 0:
                        c_.wait_ge(s_ev.h, tok_qf)
                    v_.wait_ge(s_ev.h, tok_fq1bs)
                    tok_s1 = s_sc.inc(c_.activation(s1b[:], FB[1][:], AF.Identity,
                                                    bias=fq1bs[:], scale=SCALE))
                    v_.wait_ge(s_sc.h, tok_s1)
                    v_.wait_ge(s_pe.h, tok2)
                    tok_qf = s_ev.inc(v_.scalar_tensor_tensor(qf[:, h, :], FB[2][:], fmB["fq2b"][:],
                                                              s1b[:], AluOpType.add, AluOpType.mult))
                    t_.wait_ge(s_ev.h, tok_qf)
                if sl == 0:
                    y_.wait_ge(s_cc.h, s_cc.n)
                    v_.wait_ge(s_ev.h, tok_kv0)
                    for rr in range(4):
                        tok_l = s_kvr.inc(y_.dma_start(
                            out=kvrank[:],
                            in_=cc_out[rr * 2048:(rr + 1) * 2048, :].rearrange("(h p) e -> p h e", p=128)),
                            16)
                        v_.wait_ge(s_kvr.h, tok_l)
                        tok_acc = s_ev.inc(v_.scalar_tensor_tensor(KVf[:], kvrank[:], pm[:, rr:rr + 1],
                                                                   KVf[:], AluOpType.mult, AluOpType.add))
                        y_.wait_ge(s_ev.h, tok_acc)
                    c_.wait_ge(s_ev.h, s_ev.n)
                    tok_kvbcvt[-1] = s_sc.inc(c_.copy(KVb[:], KVf[:]))
                v_.wait_ge(s_sl.h, SLAB_LOADS)
                t_.wait_ge(s_sl.h, SLAB_LOADS)
                for ci in range(CPS):
                    cg = sl * CPS + ci
                    for r in range(2):
                        for b in range(2):
                            bank = FB[b]
                            key = (cg, r, b)
                            prev = (cg, r - 1, b) if r > 0 else (cg - 1, 1, b)
                            if prev in tok_sbfb:
                                t_.wait_ge(s_ev.h, tok_sbfb[prev])
                            for i4 in range(4):
                                h = r * 8 + b * 4 + i4
                                mm = t_.matmul(bank[:, i4 * 128:(i4 + 1) * 128],
                                               kf_sl[:, h, ci * 128:(ci + 1) * 128],
                                               qf[:, h, ci * 128:(ci + 1) * 128], start=True, stop=True)
                            tok_sTb[key] = s_pe.inc(mm)
                            v_.wait_ge(s_pe.h, tok_sTb[key])
                            if (cg - 1, r, b) in tok_ob:
                                v_.wait_ge(s_pe.h, tok_ob[(cg - 1, r, b)])
                            tok_sbfb[key] = s_ev.inc(v_.tensor_tensor(
                                sbf_all[:, r * 8 + b * 4:r * 8 + b * 4 + 4, :],
                                bank[:].rearrange("p (a b) -> p a b", a=4),
                                maskf[:].unsqueeze(1).broadcast_to((128, 4, 128)), AluOpType.mult))
                        for b in range(2):
                            ob = FB[2 + r * 2 + b]
                            okey = (cg, r, b)
                            oprev = (cg - 1, r, b)
                            if oprev in tok_o1ev:
                                t_.wait_ge(s_sc.h, tok_o1ev[oprev])
                            t_.wait_ge(s_ev.h, tok_sbfb[(cg, r, b)])
                            t_.wait_ge(s_sc.h, tok_kvbcvt[sl - 1])
                            for i4 in range(4):
                                h = r * 8 + b * 4 + i4
                                t_.matmul(ob[:, i4 * 128:(i4 + 1) * 128],
                                          qf[:, h, ci * 128:(ci + 1) * 128], KVb[:, h, :],
                                          start=True, stop=False)
                                for jci in range(ci):
                                    t_.matmul(ob[:, i4 * 128:(i4 + 1) * 128],
                                              qf[:, h, ci * 128:(ci + 1) * 128], kvn[jci][:, h, :],
                                              start=False, stop=False)
                                mm = t_.matmul(ob[:, i4 * 128:(i4 + 1) * 128],
                                               sbf_all[:, h, :], vtm_sl[:, h, ci, :],
                                               start=False, stop=True)
                            tok_ob[okey] = s_pe.inc(mm)
                    for r in range(2):
                        for b in range(2):
                            c_.wait_ge(s_pe.h, tok_ob[(cg, r, b)])
                            if cg > 0 and r == 0 and b == 0:
                                c_.wait_ge(s_ev.h, tok_rmul)
                            sl4 = slice(r * 8 + b * 4, r * 8 + b * 4 + 4)
                            tok_o1ev[(cg, r, b)] = s_sc.inc(c_.copy(o1_all[:, sl4, :],
                                                                    FB[2 + r * 2 + b][:]))
                    v_.wait_ge(s_sc.h, tok_o1ev[(cg, 1, 1)])
                    if cg == 0:
                        v_.wait_ge(s_sc.h, tok_kvbcvt[-1])
                    else:
                        v_.wait_ge(s_gv.h, tok_o2[cg - 1])
                    v_.tensor_tensor(kvrank[:], o1_all[:], o1_all[:], AluOpType.mult)
                    v_.reduce_sum(ssc16[:], kvrank[:], axis=AX)
                    tok_var = s_ev.inc(v_.tensor_scalar(ssc16[:], ssc16[:], 1.0 / DK, EPS,
                                                        AluOpType.mult, AluOpType.add))
                    c_.wait_ge(s_ev.h, tok_var)
                    tok_sd = s_sc.inc(c_.sqrt(sd16[:], ssc16[:]))
                    v_.wait_ge(s_sc.h, tok_sd)
                    v_.reciprocal(rst16[:], sd16[:])
                    if cg >= 1:
                        v_.wait_ge(s_pe.h, tok_trb[(cg - 1, 1)])
                        v_.wait_ge(s_gv.h, tok_o2[cg - 1])
                    tok_rmul = s_ev.inc(v_.tensor_tensor(
                        kvrank[:], o1_all[:],
                        rst16[:].unsqueeze(2).broadcast_to((128, 16, 128)), AluOpType.mult))
                    g_.wait_ge(s_ev.h, tok_rmul)
                    if cg > 0:
                        g_.wait_ge(s_gv.h, tok_o2[cg - 1])
                    tok_o2[cg] = s_gv.inc(g_.tensor_tensor(
                        o2_all[:], kvrank[:],
                        nw_bc[:].unsqueeze(1).broadcast_to((128, 16, 128)), AluOpType.mult))
                    t_.wait_ge(s_gv.h, tok_o2[cg])
                    for tb in range(2):
                        if (cg - 1, tb) in tok_oslev:
                            t_.wait_ge(s_sc.h, tok_oslev[(cg - 1, tb)])
                        for i8 in range(8):
                            h = tb * 8 + i8
                            mm = t_.transpose(TRB[tb][:, i8 * 128:(i8 + 1) * 128], o2_all[:, h, :], idb[:])
                        tok_trb[(cg, tb)] = s_pe.inc(mm)
                        c_.wait_ge(s_pe.h, tok_trb[(cg, tb)])
                        tok_oslev[(cg, tb)] = s_sc.inc(c_.copy(
                            o_sl[:, ci, tb * 8:(tb + 1) * 8, :], TRB[tb][:]))
                g_.wait_ge(s_pe.h, tok_ob[(sl * CPS + CPS - 1, 1, 1)])
                for jci in range(CPS - 1):
                    if sl == 0 and jci == 0:
                        g_.wait_ge(s_sc.h, tok_kvbcvt[-1])
                    else:
                        g_.wait_ge(s_gv.h, s_gv.n)
                    s_gv.inc(g_.tensor_tensor(KVf[:], KVf[:], kvn[jci][:], AluOpType.add))
                y_.wait_ge(s_gv.h, s_gv.n)
                tok_k3 = s_kvl[0].inc(y_.dma_start(
                    out=kvn[0][:], in_=kv_scr[:, sl * CPS + CPS - 1, :, :].rearrange("h p e -> p h e")), 16)
                g_.wait_ge(s_kvl[0].h, tok_k3)
                g_.wait_ge(s_gv.h, s_gv.n)
                s_gv.inc(g_.tensor_tensor(KVf[:], KVf[:], kvn[0][:], AluOpType.add))
                tok_baseupd[sl] = s_gv.n
                c_.wait_ge(s_gv.h, tok_baseupd[sl])
                tok_kvbcvt[sl] = s_sc.inc(c_.copy(KVb[:], KVf[:]))
                last_osl = s_sc.n
                for m in range(DC):
                    par = m % 2
                    gi = sl * DC + m
                    wb_bank = FB[2 + par]
                    t_.wait_ge(s_wo[par].h, tok_wo[gi])
                    t_.wait_ge(s_sc.h, last_osl)
                    if gi >= 2:
                        t_.wait_ge(s_ev.h, tok_Ae[gi - 2])
                    for hh in range(H):
                        mm = t_.matmul(wb_bank[:], wob[par][:, hh, :], o_sl[:, :, hh, :],
                                       start=(hh == 0), stop=(hh == H - 1))
                    tok_A = s_pe.inc(mm)
                    gi2 = gi + 2
                    if gi2 < SLABS * DC:
                        g_.wait_ge(s_pe.h, tok_A)
                        tok_wo[gi2] = load_wo(gi2, par)
                    v_.wait_ge(s_pe.h, tok_A)
                    if gi >= 1:
                        v_.wait_ge(s_A[(gi - 1) % 2].h, tok_Ast[gi - 1])
                    tok_Ae[gi] = s_ev.inc(v_.tensor_copy(A_sl[par][:], wb_bank[:]))
                    t_.wait_ge(s_ev.h, tok_Ae[gi])
                    y_.wait_ge(s_ev.h, tok_Ae[gi])
                    tok_Ast[gi] = s_A[par].inc(y_.dma_start(
                        out=A_scr[m * 128:(m + 1) * 128, sl * T_SLAB:(sl + 1) * T_SLAB],
                        in_=A_sl[par][:]), 16)

        with ExitStack() as ph:
            pes = ph.enter_context
            p_all = pes(nc.sbuf_tensor("p_all", [128, 16, 32], F32))
            c_all = pes(nc.sbuf_tensor("c_all", [128, 16, 32], F32))
            mk_all = pes(nc.sbuf_tensor("mk_all", [128, 16, 32], F32))
            w_all = pes(nc.sbuf_tensor("w_all", [128, 16, 32], F32))
            m0a = pes(nc.sbuf_tensor("m0a", [128, 16], F32))
            mia = pes(nc.sbuf_tensor("mia", [128, 16], F32))
            s8a = pes(nc.sbuf_tensor("s8a", [128, 16], F32))
            rca = pes(nc.sbuf_tensor("rca", [128, 16], F32))
            Wexp = pes(nc.sbuf_tensor("Wexp", [32, T], F32))
            ps_rl = pes(nc.psum_tensor("ps_rl", [128, 512], F32))
            ps_rt = pes(nc.psum_tensor("ps_rt", [128, 512], F32))

            for eng in (t_, nc.vector, nc.scalar):
                eng.wait_ge(s_ev.h, s_ev.n)
                eng.wait_ge(s_sc.h, s_sc.n)
                eng.wait_ge(s_pe.h, s_pe.n)
            for tt16 in range(16):
                for dc in range(DC):
                    mm = t_.matmul(ps_rl[:, tt16 * 32:(tt16 + 1) * 32],
                                   xn_fm[:, dc, tt16 * 128:(tt16 + 1) * 128], rT[:, dc, :],
                                   start=(dc == 0), stop=(dc == DC - 1))
            tok_logits = s_pe.inc(mm)
            v_.wait_ge(s_pe.h, tok_logits)
            v_.reduce_max(m0a[:], ps_rl[:].rearrange("p (a b) -> p a b", a=16), axis=AX)
            v_.tensor_tensor(p_all[:], ps_rl[:].rearrange("p (a b) -> p a b", a=16),
                             m0a[:].unsqueeze(2).broadcast_to((128, 16, 32)), AluOpType.subtract)
            tok_sub = s_ev.inc(v_.memset(s8a[:], 0.0))
            c_.wait_ge(s_ev.h, tok_sub)
            tok_exp = s_sc.inc(c_.activation(p_all[:], p_all[:], AF.Exp))
            v_.wait_ge(s_sc.h, tok_exp)
            v_.tensor_copy(c_all[:], p_all[:])
            for it in range(K_TOP):
                v_.reduce_max(mia[:], c_all[:], axis=AX)
                v_.tensor_tensor(s8a[:], s8a[:], mia[:], AluOpType.add)
                if it < K_TOP - 1:
                    v_.tensor_tensor(mk_all[:], c_all[:],
                                     mia[:].unsqueeze(2).broadcast_to((128, 16, 32)), AluOpType.is_equal)
                    v_.scalar_tensor_tensor(c_all[:], mk_all[:], -1e30, c_all[:],
                                            AluOpType.mult, AluOpType.add)
            v_.tensor_tensor(mk_all[:], p_all[:],
                             mia[:].unsqueeze(2).broadcast_to((128, 16, 32)), AluOpType.is_ge)
            v_.reciprocal(rca[:], s8a[:])
            v_.tensor_tensor(w_all[:], p_all[:], mk_all[:], AluOpType.mult)
            tok_w = s_ev.inc(v_.tensor_tensor(w_all[:], w_all[:],
                                              rca[:].unsqueeze(2).broadcast_to((128, 16, 32)),
                                              AluOpType.mult))
            t_.wait_ge(s_ev.h, tok_w)
            tok_wev = {}
            for tt16 in range(16):
                trbank = ps_rt if tt16 % 2 == 0 else ps_rl
                if tt16 >= 2:
                    t_.wait_ge(s_ev.h, tok_wev[tt16 - 2])
                tok_trr = s_pe.inc(t_.transpose(trbank[0:32, 0:128], w_all[:, tt16, :], idf[:]))
                v_.wait_ge(s_pe.h, tok_trr)
                tok_wev[tt16] = s_ev.inc(v_.tensor_copy(
                    Wexp[:, tt16 * 128:(tt16 + 1) * 128], trbank[0:32, 0:128]))
            y_.wait_ge(s_ev.h, tok_wev[15])
            s_wsc.inc(y_.dma_start(out=w_scr, in_=Wexp[:]), 16)
            y_.wait_ge(s_wsc.h, s_wsc.n)

    with ExitStack() as ph:
        pes = ph.enter_context
        xnh = pes(nc.sbuf_tensor("xnh", [128, DC, T_HALF], BF16))
        acc = pes(nc.sbuf_tensor("acc_s", [128, DC, T_HALF], F32))
        wgs = [pes(nc.sbuf_tensor(f"wg{s}", [128, DC, I_BLK], BF16)) for s in range(2)]
        wus = [pes(nc.sbuf_tensor(f"wu{s}", [128, DC, I_BLK], BF16)) for s in range(2)]
        wds = pes(nc.sbuf_tensor("wd_s", [128, 6, D], BF16))
        hbf = pes(nc.sbuf_tensor("h_s", [128, 6, T_HALF], BF16))
        sg = [pes(nc.sbuf_tensor(f"sg{i}", [128, 512], F32)) for i in range(2)]
        t1 = [pes(nc.sbuf_tensor(f"t1{i}", [128, 512], F32)) for i in range(2)]
        wrow = [pes(nc.sbuf_tensor(f"wrow{i}", [1, 512], F32)) for i in range(2)]
        Abuf = [pes(nc.sbuf_tensor(f"Abuf{i}", [128, T_HALF], F32)) for i in range(2)]
        outt = [pes(nc.sbuf_tensor(f"outt{i}", [128, D], F32)) for i in range(2)]
        ps_g = [pes(nc.psum_tensor(f"ps_g{i}", [128, 512], F32)) for i in range(2)]
        ps_u = [pes(nc.psum_tensor(f"ps_u{i}", [128, 512], F32)) for i in range(2)]
        ps_d = [pes(nc.psum_tensor(f"ps_d{i}", [128, 512], F32)) for i in range(2)]
        ps_w = [pes(nc.psum_tensor(f"ps_w{i}", [128, 512], F32)) for i in range(2)]

        for eng in (g_, y_, nc.vector, nc.scalar, nc.tensor):
            eng.wait_ge(s_A[0].h, s_A[0].n)
            eng.wait_ge(s_A[1].h, s_A[1].n)
            eng.wait_ge(s_ev.h, s_ev.n)
            eng.wait_ge(s_sc.h, s_sc.n)
            eng.wait_ge(s_pe.h, s_pe.n)
            eng.wait_ge(s_wsc.h, 16)

        def load_gu(j, half):
            e, ib = (j // IB) % NE, j % IB
            slot = j % 2
            sem = s_gu[slot]
            src_g = wg_in[e, :, ib * I_BLK:(ib + 1) * I_BLK].rearrange("(dc p) f -> p dc f", p=128)
            src_u = wu_in[e, :, ib * I_BLK:(ib + 1) * I_BLK].rearrange("(dc p) f -> p dc f", p=128)
            sem.inc(g_.dma_start(out=wgs[slot][:], in_=src_g), 16)
            return sem.inc(g_.dma_start(out=wus[slot][:], in_=src_u), 16)

        def load_wd(e):
            src = wd_in[e].rearrange("(ic p) f -> p ic f", p=128)
            return s_wd.inc(g_.dma_start(out=wds[:], in_=src), 16)

        tok_gu_loaded = {}
        tok_wd_loaded = {}
        tok_sg = {}
        tok_u_free = {}
        tok_w_free = {}
        tok_wb = {}
        tok_h = {}
        tok_dacc = {}
        tok_wrow = {}
        step = 0
        dstep = 0
        tok_gu_loaded[0] = load_gu(0, 0)
        tok_gu_loaded[1] = load_gu(1, 0)
        tok_wd_loaded[0] = load_wd(0)

        for half in range(2):
            h0 = half * T_HALF
            if half > 0:
                y_.wait_ge(s_pe.h, s_pe.n)
            tok_xnf = s_xnf.n
            for dc in range(DC):
                s_xnf.inc(y_.dma_start(out=xnh[:, dc, :],
                                       in_=xn_scr[h0:h0 + T_HALF, dc * 128:(dc + 1) * 128],
                                       transpose=True), 16)
            XNF = s_xnf.n
            if half > 0:
                v_.wait_ge(s_pe.h, s_pe.n)

            y_.wait_ge(s_wsc.h, 16)
            t_.wait_ge(s_xnf.h, XNF)
            t_.wait_ge(s_ev.h, tok_ones)
            if half > 0:
                y_.wait_ge(s_pe.h, s_pe.n)
            for tsw in range(TSF):
                tok_wrow[(half, 0, tsw)] = s_wr[tsw].inc(y_.dma_start(
                    out=wrow[tsw][:], in_=w_scr[0:1, h0 + tsw * 512:h0 + (tsw + 1) * 512]), 16)

            for e in range(NE):
                for ib in range(IB):
                    j = half * IB * NE + IB * e + ib
                    t_.wait_ge(s_gu[j % 2].h, tok_gu_loaded[j])
                    for ts in range(TSF):
                        if ib == 0:
                            if tok_w_free.get(ts):
                                t_.wait_ge(s_ev.h, tok_w_free[ts])
                            t_.wait_ge(s_wr[ts].h, tok_wrow[(half, e, ts)])
                            mmw = t_.matmul(ps_w[ts][:], ones_row[:], wrow[ts][:],
                                            start=True, stop=True)
                            tok_wb[ts] = s_pe.inc(mmw)
                            if e + 1 < NE:
                                y_.wait_ge(s_pe.h, tok_wb[ts])
                                tok_wrow[(half, e + 1, ts)] = s_wr[ts].inc(
                                    y_.dma_start(out=wrow[ts][:],
                                                 in_=w_scr[e + 1:e + 2, h0 + ts * 512:h0 + (ts + 1) * 512]),
                                    16)
                        for ic in range(ICB):
                            par = step % 2
                            icg = ib * ICB + ic
                            if step >= 2:
                                t_.wait_ge(s_ev.h, tok_u_free[step - 2])
                            for dc in range(DC):
                                mm = t_.matmul(ps_g[par][:], wgs[j % 2][:, dc, ic * 128:(ic + 1) * 128],
                                               xnh[:, dc, ts * 512:(ts + 1) * 512],
                                               start=(dc == 0), stop=(dc == DC - 1))
                            tok_gmm = s_pe.inc(mm)
                            for dc in range(DC):
                                mm = t_.matmul(ps_u[par][:], wus[j % 2][:, dc, ic * 128:(ic + 1) * 128],
                                               xnh[:, dc, ts * 512:(ts + 1) * 512],
                                               start=(dc == 0), stop=(dc == DC - 1))
                            tok_umm = s_pe.inc(mm)
                            c_.wait_ge(s_pe.h, tok_gmm)
                            if step >= 2:
                                c_.wait_ge(s_ev.h, tok_u_free[step - 2])
                            tok_sg[step] = s_sc.inc(c_.activation(sg[par][:], ps_g[par][:], AF.Silu))
                            v_.wait_ge(s_sc.h, tok_sg[step])
                            v_.wait_ge(s_pe.h, tok_umm)
                            v_.wait_ge(s_pe.h, tok_wb[ts])
                            v_.tensor_tensor(t1[par][:], sg[par][:], ps_u[par][:], AluOpType.mult)
                            tok_u_free[step] = s_ev.inc(
                                v_.tensor_tensor(hbf[:, icg, ts * 512:(ts + 1) * 512], t1[par][:],
                                                 ps_w[ts][:], AluOpType.mult))
                            if (ib, ic) == (1, ICB - 1):
                                tok_h[ts] = tok_u_free[step]
                                tok_w_free[ts] = tok_u_free[step]
                            step += 1
                    tok_gu_done = s_pe.n
                    if j + 2 < 2 * IB * NE:
                        g_.wait_ge(s_pe.h, tok_gu_done)
                        tok_gu_loaded[j + 2] = load_gu(j + 2, half)
                ge = half * NE + e
                t_.wait_ge(s_wd.h, tok_wd_loaded[ge])
                for ts in range(TSF):
                    t_.wait_ge(s_ev.h, tok_h[ts])
                    for m in range(DC):
                        par = dstep % 2
                        if dstep >= 2:
                            t_.wait_ge(s_ev.h, tok_dacc[dstep - 2])
                        for ic in range(6):
                            mm = t_.matmul(ps_d[par][:], wds[:, ic, m * 128:(m + 1) * 128],
                                           hbf[:, ic, ts * 512:(ts + 1) * 512],
                                           start=(ic == 0), stop=(ic == 5))
                        tok_dm = s_pe.inc(mm)
                        v_.wait_ge(s_pe.h, tok_dm)
                        if e == 0:
                            tok_dacc[dstep] = s_ev.inc(
                                v_.tensor_copy(acc[:, m, ts * 512:(ts + 1) * 512], ps_d[par][:]))
                        else:
                            tok_dacc[dstep] = s_ev.inc(
                                v_.tensor_tensor(acc[:, m, ts * 512:(ts + 1) * 512],
                                                 acc[:, m, ts * 512:(ts + 1) * 512],
                                                 ps_d[par][:], AluOpType.add))
                        dstep += 1
                tok_d_done = tok_dacc[dstep - 1]
                if ge + 1 < 2 * NE:
                    g_.wait_ge(s_pe.h, s_pe.n)
                    tok_wd_loaded[ge + 1] = load_wd((ge + 1) % NE)

            tok_Al = {}
            for mi in range(2):
                tok_Al[mi] = s_Al[mi].inc(y_.dma_start(
                    out=Abuf[mi][:], in_=A_scr[mi * 128:(mi + 1) * 128, h0:h0 + T_HALF]), 16)
            tok_accA = {}
            for m in range(DC):
                apar = m % 2
                v_.wait_ge(s_Al[apar].h, tok_Al[m])
                v_.wait_ge(s_ev.h, tok_d_done)
                tok_accA[m] = s_ev.inc(v_.tensor_tensor(acc[:, m, :], acc[:, m, :], Abuf[apar][:],
                                                        AluOpType.add))
                if m + 2 < DC:
                    y_.wait_ge(s_ev.h, tok_accA[m])
                    tok_Al[m + 2] = s_Al[apar].inc(y_.dma_start(
                        out=Abuf[apar][:], in_=A_scr[(m + 2) * 128:(m + 3) * 128, h0:h0 + T_HALF]), 16)
            ACCA = s_ev.n
            y_.wait_ge(s_out[0].h, s_out[0].n)
            y_.wait_ge(s_out[1].h, s_out[1].n)
            tok_xo = {0: s_xo[0].inc(y_.dma_start(out=outt[0][:], in_=x_in[h0:h0 + 128, :]), 16),
                      1: s_xo[1].inc(y_.dma_start(out=outt[1][:], in_=x_in[h0 + 128:h0 + 256, :]), 16)}
            tok_outst = {}
            tok_outadd = {}
            for ttile in range(8):
                opar = ttile % 2
                t_.wait_ge(s_ev.h, ACCA)
                for mg in range(4):
                    bank = ps_d[mg % 2]
                    gidx = ttile * 4 + mg
                    if gidx >= 2:
                        t_.wait_ge(s_ev.h, tok_outadd[gidx - 2])
                    for mi4 in range(4):
                        m = mg * 4 + mi4
                        mm = t_.transpose(bank[:, mi4 * 128:(mi4 + 1) * 128],
                                          acc[:, m, ttile * 128:(ttile + 1) * 128], idf[:])
                    tok_trm = s_pe.inc(mm)
                    v_.wait_ge(s_pe.h, tok_trm)
                    if mg == 0:
                        v_.wait_ge(s_xo[opar].h, tok_xo[ttile])
                    tok_outadd[gidx] = s_ev.inc(
                        v_.tensor_tensor(outt[opar][:, mg * 512:(mg + 1) * 512],
                                         outt[opar][:, mg * 512:(mg + 1) * 512], bank[:],
                                         AluOpType.add))
                y_.wait_ge(s_ev.h, tok_outadd[ttile * 4 + 3])
                tok_outst[ttile] = s_out[opar].inc(y_.dma_start(
                    out=out_dram[h0 + ttile * 128:h0 + (ttile + 1) * 128, :], in_=outt[opar][:]), 16)
                if ttile + 2 < 8:
                    y_.wait_ge(s_out[opar].h, tok_outst[ttile])
                    tok_xo[ttile + 2] = s_xo[opar].inc(y_.dma_start(
                        out=outt[opar][:], in_=x_in[h0 + (ttile + 2) * 128:h0 + (ttile + 3) * 128, :]), 16)
        y_.wait_ge(s_out[0].h, s_out[0].n)
        y_.wait_ge(s_out[1].h, s_out[1].n)

    stack.close()
    return nc


_NC_CACHE = None


def kernel(L0_attn_x, L0_attn_wq, L0_attn_wk, L0_attn_wv, L0_attn_wo,
           L0_attn_fq1_w, L0_attn_fq1_b, L0_attn_fq2_w, L0_attn_fq2_b,
           L0_attn_fk1_w, L0_attn_fk1_b, L0_attn_fk2_w, L0_attn_fk2_b,
           L0_attn_norm_w, L0_ffn_router_w, L0_ffn_w_gate, L0_ffn_w_up, L0_ffn_w_down):
    global _NC_CACHE
    f32 = np.float32
    A = lambda a: np.ascontiguousarray(np.asarray(a, dtype=f32))
    x = A(L0_attn_x)
    common = {
        "wqT": A(np.asarray(L0_attn_wq).T), "wkT": A(np.asarray(L0_attn_wk).T),
        "wvT": A(np.asarray(L0_attn_wv).T), "woT": A(np.asarray(L0_attn_wo).T),
        "fq1T": A(np.asarray(L0_attn_fq1_w).T), "fq2T": A(np.asarray(L0_attn_fq2_w).T),
        "fk1T": A(np.asarray(L0_attn_fk1_w).T), "fk2T": A(np.asarray(L0_attn_fk2_w).T),
        "fq1b": A(L0_attn_fq1_b).reshape(128, 1), "fq2b": A(L0_attn_fq2_b).reshape(128, 1),
        "fk1b": A(L0_attn_fk1_b).reshape(128, 1), "fk2b": A(L0_attn_fk2_b).reshape(128, 1),
        "nw_bc": A(np.broadcast_to(np.asarray(L0_attn_norm_w, dtype=f32).reshape(1, 128), (128, 128))),
        "maskf": np.triu(np.ones((128, 128), f32)),
        "id_bf": np.eye(128, dtype=ml_dtypes.bfloat16),
        "id_f32": np.eye(128, dtype=f32),
        "routerT": A(np.asarray(L0_ffn_router_w).T),
        "wg": A(L0_ffn_w_gate), "wu": A(L0_ffn_w_up), "wd": A(L0_ffn_w_down),
    }
    in_maps = []
    for core in range(8):
        b, qt = core // 4, core % 4
        pmv = np.zeros((128, 4), f32)
        pmv[:, :qt] = 1.0
        in_maps.append({**common, "x": np.ascontiguousarray(x[b, qt * T:(qt + 1) * T]), "pm": pmv})

    if _NC_CACHE is None:
        _NC_CACHE = build()
    res = run_bass_kernel_spmd(_NC_CACHE, in_maps, core_ids=list(range(8)))
    out = np.empty((2, 8192, D), f32)
    for core in range(8):
        b, qt = core // 4, core % 4
        out[b, qt * T:(qt + 1) * T] = res.results[core]["out"]
    return out


# version 10
# speedup vs baseline: 1.0128x; 1.0128x over previous
import numpy as np
import ml_dtypes
import concourse.bass as bass
import concourse.mybir as mybir
from concourse.alu_op_type import AluOpType
from concourse.bass_utils import run_bass_kernel_spmd
from contextlib import ExitStack

F32 = mybir.dt.float32
BF16 = mybir.dt.bfloat16
AF = mybir.ActivationFunctionType
AX = mybir.AxisListType.X

D, DC, H, DK = 2048, 16, 16, 128
T, NCH, SLABS, T_SLAB, CPS = 2048, 16, 4, 512, 4
EPS = 1e-5
SCALE = DK ** -0.5
RG = [[0, 1, 2, 3], [4, 5, 6, 7]]
NE, I_FULL, IB, I_BLK, ICB = 32, 768, 3, 256, 2
T_HALF, TSF, TT, K_TOP = 1024, 2, 8, 8

DATAPATH_OPS = {
    'tensor_tensor', 'tensor_scalar', 'tensor_single_scalar', 'tensor_copy',
    'tensor_scalar_mul', 'tensor_scalar_add', 'tensor_add', 'tensor_mul', 'tensor_sub',
    'tensor_reduce', 'reduce_max', 'reduce_sum', 'reciprocal', 'memset', 'select',
    'scalar_tensor_tensor', 'activation', 'copy', 'mul', 'add', 'sqrt', 'square', 'iota',
    'tensor_tensor_scan', 'bn_stats', 'bn_aggr', 'transpose',
}


class SerialEng:
    def __init__(self, eng):
        self._e = eng

    def __getattr__(self, name):
        f = getattr(self._e, name)
        if name in DATAPATH_OPS:
            def wrapped(*a, **k):
                r = f(*a, **k)
                self._e.drain()
                return r
            return wrapped
        return f


class Sem:
    def __init__(self, es, nc, name):
        self.h = es(nc.semaphore(name))
        self.n = 0

    def inc(self, instr, k=1):
        instr.then_inc(self.h, k)
        self.n += k
        return self.n


def build():
    nc = bass.Bass("TRN2", target_bir_lowering=False)
    ap = lambda name, shape, dt: nc.dram_tensor(name, shape, dt, kind="ExternalInput").ap()
    x_in = ap("x", [T, D], F32)
    wqT_in = ap("wqT", [D, D], F32)
    wkT_in = ap("wkT", [D, D], F32)
    wvT_in = ap("wvT", [D, D], F32)
    woT_in = ap("woT", [D, D], F32)
    fm_ins = {n: ap(n, [128, 128], F32) for n in ("fq1T", "fq2T", "fk1T", "fk2T")}
    b_ins = {n: ap(n, [128, 1], F32) for n in ("fq1b", "fq2b", "fk1b", "fk2b")}
    nw_bc_in = ap("nw_bc", [128, 128], F32)
    mask_in = ap("maskf", [128, 128], F32)
    idb_in = nc.dram_tensor("id_bf", [128, 128], BF16, kind="ExternalInput").ap()
    idf_in = ap("id_f32", [128, 128], F32)
    pm_in = ap("pm", [128, 4], F32)
    routerT_in = ap("routerT", [D, 32], F32)
    wg_in = ap("wg", [NE, D, I_FULL], F32)
    wu_in = ap("wu", [NE, D, I_FULL], F32)
    wd_in = ap("wd", [NE, I_FULL, D], F32)

    out_dram = nc.dram_tensor("out", [T, D], F32, kind="ExternalOutput").ap()

    xn_scr = nc.dram_tensor("xn_scr", [T, D], BF16).ap()
    kf_scr = nc.dram_tensor("kf_scr", [D, T], BF16).ap()
    vtm_scr = nc.dram_tensor("vtm_scr", [H, T, DK], BF16).ap()
    kv_scr = nc.dram_tensor("kv_scr", [H, NCH, 128, DK], BF16).ap()
    A_scr = nc.dram_tensor("A_scr", [D, T], F32).ap()
    w_scr = nc.dram_tensor("w_scr", [32, T], F32).ap()
    cc_in_h = nc.dram_tensor("cc_in", [H * 128, DK], F32)
    cc_out_h = nc.dram_tensor("cc_out", [4 * H * 128, DK], F32)
    cc_in, cc_out = cc_in_h.ap(), cc_out_h.ap()

    stack = ExitStack()
    es = stack.enter_context

    s_x = [Sem(es, nc, f"s_x{i}") for i in range(2)]
    s_boot = Sem(es, nc, "s_boot")
    s_bootg = Sem(es, nc, "s_bootg")
    s_xnst = Sem(es, nc, "s_xnst")
    s_xnfm = Sem(es, nc, "s_xnfm")
    s_wk = [Sem(es, nc, f"s_wk{i}") for i in range(2)]
    s_wv = [Sem(es, nc, f"s_wv{i}") for i in range(2)]
    s_wq = [Sem(es, nc, f"s_wq{i}") for i in range(2)]
    s_wo = [Sem(es, nc, f"s_wo{i}") for i in range(2)]
    s_kf2 = [Sem(es, nc, f"s_kf{i}") for i in range(2)]
    s_vt2 = [Sem(es, nc, f"s_vt{i}") for i in range(2)]
    s_kvn = [Sem(es, nc, f"s_kvn{i}") for i in range(2)]
    s_cci = Sem(es, nc, "s_cci")
    s_A = [Sem(es, nc, f"s_A{i}") for i in range(2)]
    s_cc = Sem(es, nc, "s_cc")
    s_kvr = Sem(es, nc, "s_kvr")
    s_kvl = [Sem(es, nc, f"s_kvl{i}") for i in range(2)]
    s_sl = Sem(es, nc, "s_sl")
    s_gu = [Sem(es, nc, f"s_gu{i}") for i in range(2)]
    s_wd = Sem(es, nc, "s_wd")
    s_wsc = Sem(es, nc, "s_wsc")
    s_wr = [Sem(es, nc, f"s_wr{i}") for i in range(2)]
    s_xnf = Sem(es, nc, "s_xnf")
    s_Al = [Sem(es, nc, f"s_Al{i}") for i in range(2)]
    s_xo = [Sem(es, nc, f"s_xo{i}") for i in range(2)]
    s_out = [Sem(es, nc, f"s_out{i}") for i in range(2)]
    s_pe = Sem(es, nc, "s_pe")
    s_sc = Sem(es, nc, "s_sc")
    s_ev = Sem(es, nc, "s_ev")
    s_gv = Sem(es, nc, "s_gv")

    idf = es(nc.sbuf_tensor("idf_s", [128, 128], F32))
    rT = es(nc.sbuf_tensor("rT_s", [128, DC, 32], BF16))
    ones_row = es(nc.sbuf_tensor("ones_row", [1, 128], F32))

    block = es(nc.Block())
    t_ = nc.tensor
    v_ = SerialEng(nc.vector)
    c_ = SerialEng(nc.scalar)
    g_ = nc.gpsimd
    y_ = nc.sync

    with ExitStack() as aph:
        aes = aph.enter_context
        xn_fm = aes(nc.sbuf_tensor("xn_fm", [128, DC, T], BF16))
        KVf = aes(nc.sbuf_tensor("KVf", [128, H, DK], F32))
        KVb = aes(nc.sbuf_tensor("KVb", [128, H, DK], BF16))
        fmT = {n: aes(nc.sbuf_tensor(n + "_s", [128, 128], BF16)) for n in ("fq1T", "fq2T", "fk1T", "fk2T")}
        fmB = {n: aes(nc.sbuf_tensor(n + "_s", [128, 1], F32)) for n in ("fq1b", "fq2b", "fk1b", "fk2b")}
        kvrank = aes(nc.sbuf_tensor("kvrank", [128, H, DK], F32))
        fq1bs = aes(nc.sbuf_tensor("fq1bs", [128, 1], F32))
        nw_bc = aes(nc.sbuf_tensor("nw_bc_s", [128, 128], F32))
        maskf = aes(nc.sbuf_tensor("maskf_s", [128, 128], F32))
        idb = aes(nc.sbuf_tensor("idb_s", [128, 128], BF16))
        pm = aes(nc.sbuf_tensor("pm_s", [128, 4], F32))

        for dst, srcp in [(nw_bc, nw_bc_in), (maskf, mask_in), (pm, pm_in), (idb, idb_in),
                          (idf, idf_in)] + [(fmB[n], b_ins[n]) for n in fmB]:
            s_boot.inc(y_.dma_start(out=dst[:], in_=srcp), 16)
        for n in fmT:
            s_bootg.inc(g_.dma_start(out=fmT[n][:], in_=fm_ins[n]), 16)
        s_bootg.inc(g_.dma_start(out=rT[:], in_=routerT_in.rearrange("(dc p) e -> p dc e", p=128)), 16)
        BOOT, BOOTG = s_boot.n, s_bootg.n
        v_.wait_ge(s_boot.h, BOOT)
        tok_fq1bs = s_ev.inc(v_.tensor_scalar_mul(fq1bs[:], fmB["fq1b"][:], SCALE))
        tok_kv0 = s_ev.inc(v_.memset(KVf[:], 0.0))
        tok_ones = s_ev.inc(v_.memset(ones_row[:], 1.0))

        with ExitStack() as ph:
            pes = ph.enter_context
            xt = [pes(nc.sbuf_tensor(f"xt{i}", [128, D], F32)) for i in range(2)]
            xc = [pes(nc.sbuf_tensor(f"xc{i}", [128, D], F32)) for i in range(2)]
            sqt = pes(nc.sbuf_tensor("sqt", [128, D], F32))
            xnt = [pes(nc.sbuf_tensor(f"xnt{i}", [128, D], BF16)) for i in range(2)]
            mcol = pes(nc.sbuf_tensor("mcol", [128, 1], F32))
            vcol = pes(nc.sbuf_tensor("vcol", [128, 1], F32))
            sdcol = pes(nc.sbuf_tensor("sdcol", [128, 1], F32))
            rstd = pes(nc.sbuf_tensor("rstd", [128, 1], F32))

            tok_x = {0: s_x[0].inc(y_.dma_start(out=xt[0][:], in_=x_in[0:128, :]), 16),
                     1: s_x[1].inc(y_.dma_start(out=xt[1][:], in_=x_in[128:256, :]), 16)}
            tok_xnst = {}
            for tt in range(NCH):
                par = tt % 2
                v_.wait_ge(s_x[par].h, tok_x[tt])
                v_.reduce_sum(mcol[:], xt[par][:], axis=AX)
                v_.tensor_scalar_mul(mcol[:], mcol[:], 1.0 / D)
                tok_xc = s_ev.inc(v_.tensor_scalar(xc[par][:], xt[par][:], mcol[:], None, AluOpType.subtract))
                c_.wait_ge(s_ev.h, tok_xc)
                tok_sq = s_sc.inc(c_.square(sqt[:], xc[par][:]))
                v_.wait_ge(s_sc.h, tok_sq)
                v_.reduce_sum(vcol[:], sqt[:], axis=AX)
                tok_var = s_ev.inc(v_.tensor_scalar(vcol[:], vcol[:], 1.0 / D, EPS,
                                                    AluOpType.mult, AluOpType.add))
                c_.wait_ge(s_ev.h, tok_var)
                tok_sd = s_sc.inc(c_.sqrt(sdcol[:], vcol[:]))
                v_.wait_ge(s_sc.h, tok_sd)
                v_.reciprocal(rstd[:], sdcol[:])
                tok_xn = s_ev.inc(v_.tensor_scalar(xnt[par][:], xc[par][:], rstd[:], None, AluOpType.mult))
                y_.wait_ge(s_ev.h, tok_xn)
                tok_xnst[tt] = s_xnst.inc(
                    y_.dma_start(out=xn_scr[tt * 128:(tt + 1) * 128, :], in_=xnt[par][:]), 16)
                if tt + 2 < NCH:
                    y_.wait_ge(s_xnst.h, tok_xnst[tt])
                    tok_x[tt + 2] = s_x[par].inc(
                        y_.dma_start(out=xt[par][:], in_=x_in[(tt + 2) * 128:(tt + 3) * 128, :]), 16)
            y_.wait_ge(s_xnst.h, s_xnst.n)
            for dc in range(DC):
                s_xnfm.inc(y_.dma_start(out=xn_fm[:, dc, :], in_=xn_scr[:, dc * 128:(dc + 1) * 128],
                                        transpose=True), 16)
            XNFM = s_xnfm.n

        with ExitStack() as ph:
            pes = ph.enter_context
            wkb = [pes(nc.sbuf_tensor(f"wkb{i}", [128, DC, 128], BF16)) for i in range(2)]
            wvb = [pes(nc.sbuf_tensor(f"wvb{i}", [128, DC, 128], BF16)) for i in range(2)]
            kraw_ = [pes(nc.sbuf_tensor(f"kraw{i}", [128, T], BF16)) for i in range(2)]
            vraw_ = [pes(nc.sbuf_tensor(f"vraw{i}", [128, T], BF16)) for i in range(2)]
            s1b = pes(nc.sbuf_tensor("s1b", [128, T_SLAB], F32))
            s2b = pes(nc.sbuf_tensor("s2b", [128, T_SLAB], F32))
            kfb_ = [pes(nc.sbuf_tensor(f"kfb{i}", [128, T], BF16)) for i in range(2)]
            ktm_ = [pes(nc.sbuf_tensor(f"ktm{i}", [128, NCH, 128], BF16)) for i in range(2)]
            vtm_ = [pes(nc.sbuf_tensor(f"vtm{i}", [128, NCH, 128], BF16)) for i in range(2)]
            kvn_h = pes(nc.sbuf_tensor("kvn_h", [128, NCH, 128], BF16))
            kvt_s = pes(nc.sbuf_tensor("kvt_s", [128, H, DK], F32))
            ps_a = [pes(nc.psum_tensor(f"a1_psa{i}", [128, T_SLAB], F32)) for i in range(2)]
            ps_t1 = pes(nc.psum_tensor("a1_pst1", [128, 1024], BF16))
            ps_t2 = pes(nc.psum_tensor("a1_pst2", [128, 1024], BF16))
            ps_kv = [pes(nc.psum_tensor(f"a1_pskv{i}", [128, 512], F32)) for i in range(2)]
            ps_kvt = pes(nc.psum_tensor("a1_pskvt", [128, 512], F32))

            def load_w(w_in, dst, sem, h):
                src = w_in[:, h * 128:(h + 1) * 128].rearrange("(dc p) m -> p dc m", p=128)
                return sem.inc(g_.dma_start(out=dst[:], in_=src), 16)

            g_.wait_ge(s_xnst.h, s_xnst.n)
            tok_wk = {h: load_w(wkT_in, wkb[h], s_wk[h], h) for h in range(2)}
            tok_wv = {h: load_w(wvT_in, wvb[h], s_wv[h], h) for h in range(2)}

            t_.wait_ge(s_xnfm.h, XNFM)
            t_.wait_ge(s_bootg.h, BOOTG)
            t_.wait_ge(s_boot.h, BOOT)
            tok_kfst = {}
            tok_vtmst = {}
            tok_kvst = {}
            tok_prjev = {}
            tok_kraw = {}
            prj = 0

            def emit_proj(h):
                par = h % 2
                t_.wait_ge(s_wk[par].h, tok_wk[h])
                t_.wait_ge(s_wv[par].h, tok_wv[h])
                for wb, raw in [(wkb[par], kraw_[par]), (wvb[par], vraw_[par])]:
                    for sl in range(SLABS):
                        p2 = pstate["prj"] % 2
                        if pstate["prj"] >= 2:
                            t_.wait_ge(s_sc.h, tok_prjev[pstate["prj"] - 2])
                        for dc in range(DC):
                            mm = t_.matmul(ps_a[p2][:], wb[:, dc, :],
                                           xn_fm[:, dc, sl * T_SLAB:(sl + 1) * T_SLAB],
                                           start=(dc == 0), stop=(dc == DC - 1))
                        tok = s_pe.inc(mm)
                        c_.wait_ge(s_pe.h, tok)
                        tok_prjev[pstate["prj"]] = s_sc.inc(
                            c_.copy(raw[:, sl * T_SLAB:(sl + 1) * T_SLAB], ps_a[p2][:]))
                        pstate["prj"] += 1
                tok_kraw[h] = tok_prjev[pstate["prj"] - 1]
                if h + 2 < H:
                    g_.wait_ge(s_sc.h, tok_kraw[h])
                    tok_wk[h + 2] = load_w(wkT_in, wkb[par], s_wk[par], h + 2)
                    tok_wv[h + 2] = load_w(wvT_in, wvb[par], s_wv[par], h + 2)

            def emit_rest(h):
                par = h % 2
                kfb = kfb_[par]
                vtm = vtm_[par]
                ktm = ktm_[par]
                kraw = kraw_[par]
                vraw = vraw_[par]
                if h > 1:
                    v_.wait_ge(s_kf2[par].h, tok_kfst[h - 2])
                    v_.wait_ge(s_vt2[par].h, tok_vtmst[h - 2])
                t_.wait_ge(s_sc.h, tok_kraw[h])
                for sl in range(SLABS):
                    tok1 = s_pe.inc(t_.matmul(ps_kv[0][:], fmT["fk1T"][:],
                                              kraw[:, sl * T_SLAB:(sl + 1) * T_SLAB], start=True, stop=True))
                    tok2 = s_pe.inc(t_.matmul(ps_kv[1][:], fmT["fk2T"][:],
                                              kraw[:, sl * T_SLAB:(sl + 1) * T_SLAB], start=True, stop=True))
                    c_.wait_ge(s_pe.h, tok1)
                    tok_s1 = s_sc.inc(c_.activation(s1b[:], ps_kv[0][:], AF.Identity,
                                                    bias=fmB["fk1b"][:], scale=1.0))
                    v_.wait_ge(s_sc.h, tok_s1)
                    v_.wait_ge(s_pe.h, tok2)
                    tok_kf = s_ev.inc(v_.scalar_tensor_tensor(kfb[:, sl * T_SLAB:(sl + 1) * T_SLAB],
                                                              ps_kv[1][:], fmB["fk2b"][:], s1b[:],
                                                              AluOpType.add, AluOpType.mult))
                    t_.wait_ge(s_ev.h, tok_kf)
                y_.wait_ge(s_ev.h, s_ev.n)
                tok_kfst[h] = s_kf2[par].inc(
                    y_.dma_start(out=kf_scr[h * 128:(h + 1) * 128, :], in_=kfb[:]), 16)
                trstep = 0
                tok_trev = {}
                for c in range(NCH):
                    for srcbuf, dstbuf in ((kfb, ktm), (vraw, vtm)):
                        bank = ps_t1 if trstep % 2 == 0 else ps_t2
                        if trstep >= 2:
                            t_.wait_ge(s_ev.h, tok_trev[trstep - 2])
                        tok_tr = s_pe.inc(t_.transpose(bank[:, 0:128],
                                                       srcbuf[:, c * 128:(c + 1) * 128], idb[:]))
                        v_.wait_ge(s_pe.h, tok_tr)
                        tok_trev[trstep] = s_ev.inc(v_.tensor_copy(dstbuf[:, c, :], bank[:, 0:128]))
                        trstep += 1
                t_.wait_ge(s_ev.h, tok_trev[trstep - 1])
                t_.wait_ge(s_ev.h, tok_trev[trstep - 2])
                y_.wait_ge(s_ev.h, s_ev.n)
                tok_vtmst[h] = s_vt2[par].inc(
                    y_.dma_start(out=vtm_scr[h].rearrange("(c p) e -> p c e", p=128), in_=vtm[:]), 16)
                if h > 0:
                    c_.wait_ge(s_kvn[0].h, tok_kvst.get(0, 0))
                tok_kvev = {}
                for c in range(NCH):
                    pp = c % 2
                    if c >= 2:
                        t_.wait_ge(s_sc.h, tok_kvev[c - 2])
                    tokn = s_pe.inc(t_.matmul(ps_kv[pp][:, 0:128], ktm[:, c, :], vtm[:, c, :],
                                              start=True, stop=True))
                    s_pe.inc(t_.matmul(ps_kvt[:, 0:128], ktm[:, c, :], vtm[:, c, :],
                                       start=(c == 0), stop=(c == NCH - 1)))
                    c_.wait_ge(s_pe.h, tokn)
                    tok_kvev[c] = s_sc.inc(c_.copy(kvn_h[:, c, :], ps_kv[pp][:, 0:128]))
                y_.wait_ge(s_sc.h, tok_kvev[NCH - 1])
                tok_kvst[0] = s_kvn[0].inc(y_.dma_start(
                    out=kv_scr[h].rearrange("c p e -> p c e"), in_=kvn_h[:]), 16)
                c_.wait_ge(s_pe.h, s_pe.n)
                tok_kvt = s_sc.inc(c_.copy(kvt_s[:, h, :], ps_kvt[:, 0:128]))
                t_.wait_ge(s_sc.h, tok_kvt)

            pstate = {"prj": 0}
            for h in range(H + 1):
                if h < H:
                    emit_proj(h)
                if h >= 1:
                    emit_rest(h - 1)
            y_.wait_ge(s_sc.h, s_sc.n)
            tok_ccst = s_cci.inc(y_.dma_start(out=cc_in.rearrange("(h p) e -> p h e", p=128),
                                              in_=kvt_s[:]), 16)
            g_.wait_ge(s_cci.h, tok_ccst)
            s_cc.inc(g_.collective_compute("AllGather", mybir.AluOpType.bypass, replica_groups=RG,
                                           ins=[cc_in_h.ap().opt()], outs=[cc_out_h.ap().opt()]))
            A1_SC_DONE = s_sc.n

        with ExitStack() as ph:
            pes = ph.enter_context
            wqb = [pes(nc.sbuf_tensor(f"wqb{i}", [128, DC, 128], BF16)) for i in range(2)]
            wob = [pes(nc.sbuf_tensor(f"wob{i}", [128, H, 128], BF16)) for i in range(2)]
            qraw = pes(nc.sbuf_tensor("qraw", [128, T_SLAB], BF16))
            s1b = pes(nc.sbuf_tensor("a2s1", [128, T_SLAB], F32))
            qf = pes(nc.sbuf_tensor("qf", [128, H, T_SLAB], BF16))
            kf_sl = pes(nc.sbuf_tensor("kf_sl", [128, H, T_SLAB], BF16))
            vtm_sl = pes(nc.sbuf_tensor("vtm_sl", [128, H, CPS, 128], BF16))
            kvn = [pes(nc.sbuf_tensor(f"kvn{i}", [128, H, DK], BF16)) for i in range(3)]
            sbf_all = pes(nc.sbuf_tensor("sbf_all", [128, H, 128], BF16))
            o1_all = pes(nc.sbuf_tensor("o1_all", [128, H, 128], F32))
            o2_all = pes(nc.sbuf_tensor("o2_all", [128, H, 128], BF16))
            ssc16 = pes(nc.sbuf_tensor("ssc16", [128, 16], F32))
            sd16 = pes(nc.sbuf_tensor("sd16", [128, 16], F32))
            rst16 = pes(nc.sbuf_tensor("rst16", [128, 16], F32))
            o_sl = pes(nc.sbuf_tensor("o_sl", [128, CPS, H, 128], BF16))
            A_sl = [pes(nc.sbuf_tensor("A_sl0", [128, T_SLAB], F32))] * 2
            FB = [pes(nc.psum_tensor(f"a2_f{i}", [128, 512], F32)) for i in range(6)]
            TRB = [pes(nc.psum_tensor(f"a2_tr{i}", [128, 1024], BF16)) for i in range(2)]

            def load_wq(h, slot):
                src = wqT_in[:, h * 128:(h + 1) * 128].rearrange("(dc p) m -> p dc m", p=128)
                return s_wq[slot].inc(g_.dma_start(out=wqb[slot][:], in_=src), 16)

            def load_wo(gi, slot):
                m = gi % DC
                src = woT_in[:, m * 128:(m + 1) * 128].rearrange("(hh p) m -> p hh m", p=128)
                return s_wo[slot].inc(g_.dma_start(out=wob[slot][:], in_=src), 16)

            g_.wait_ge(s_sc.h, A1_SC_DONE)
            y_.wait_ge(s_sc.h, A1_SC_DONE)
            g_.wait_ge(s_cci.h, 16)
            y_.wait_ge(s_cci.h, 16)
            for i in range(2):
                y_.wait_ge(s_kf2[i].h, s_kf2[i].n)
                y_.wait_ge(s_vt2[i].h, s_vt2[i].n)
                g_.wait_ge(s_kf2[i].h, s_kf2[i].n)
                g_.wait_ge(s_vt2[i].h, s_vt2[i].n)
            y_.wait_ge(s_kvn[0].h, s_kvn[0].n)
            tok_wq = {0: load_wq(0, 0), 1: load_wq(1, 1)}
            tok_wo = {0: load_wo(0, 0), 1: load_wo(1, 1)}
            tok_kvl = {}
            tok_kvbcvt = {}
            tok_sTb = {}
            tok_sbfb = {}
            tok_ob = {}
            tok_o1ev = {}
            tok_o2 = {}
            tok_trb = {}
            tok_oslev = {}
            tok_Ae = {}
            tok_Ast = {}
            tok_qf = 0

            tok_baseupd = {-1: 0}
            for sl in range(SLABS):
                if sl > 0:
                    y_.wait_ge(s_pe.h, s_pe.n)
                    v_.wait_ge(s_pe.h, s_pe.n)
                    t_.wait_ge(s_ev.h, s_ev.n)
                    t_.wait_ge(s_sc.h, s_sc.n)
                for h in range(H):
                    s_sl.inc(y_.dma_start(
                        out=kf_sl[:, h, :],
                        in_=kf_scr[h * 128:(h + 1) * 128, sl * T_SLAB:(sl + 1) * T_SLAB]), 16)
                    s_sl.inc(y_.dma_start(
                        out=vtm_sl[:, h, :, :],
                        in_=vtm_scr[h, sl * T_SLAB:(sl + 1) * T_SLAB, :].rearrange("(c p) e -> p c e", p=128)),
                        16)
                if sl > 0:
                    y_.wait_ge(s_gv.h, tok_baseupd[sl - 1])
                for ci in range(CPS - 1):
                    s_sl.inc(y_.dma_start(
                        out=kvn[ci][:], in_=kv_scr[:, sl * CPS + ci, :, :].rearrange("h p e -> p h e")), 16)
                SLAB_LOADS = s_sl.n
                for h in range(H):
                    par = h % 2
                    t_.wait_ge(s_wq[par].h, tok_wq[sl * H + h])
                    for dc in range(DC):
                        mm = t_.matmul(FB[0][:], wqb[par][:, dc, :],
                                       xn_fm[:, dc, sl * T_SLAB:(sl + 1) * T_SLAB],
                                       start=(dc == 0), stop=(dc == DC - 1))
                    tok_q = s_pe.inc(mm)
                    c_.wait_ge(s_pe.h, tok_q)
                    tok_qr = s_sc.inc(c_.copy(qraw[:], FB[0][:]))
                    t_.wait_ge(s_sc.h, tok_qr)
                    gq = sl * H + h + 2
                    if gq < SLABS * H:
                        g_.wait_ge(s_sc.h, tok_qr)
                        tok_wq[gq] = load_wq(gq % H, par)
                    tok1 = s_pe.inc(t_.matmul(FB[1][:], fmT["fq1T"][:], qraw[:], start=True, stop=True))
                    tok2 = s_pe.inc(t_.matmul(FB[2][:], fmT["fq2T"][:], qraw[:], start=True, stop=True))
                    c_.wait_ge(s_pe.h, tok1)
                    if sl * H + h > 0:
                        c_.wait_ge(s_ev.h, tok_qf)
                    v_.wait_ge(s_ev.h, tok_fq1bs)
                    tok_s1 = s_sc.inc(c_.activation(s1b[:], FB[1][:], AF.Identity,
                                                    bias=fq1bs[:], scale=SCALE))
                    v_.wait_ge(s_sc.h, tok_s1)
                    v_.wait_ge(s_pe.h, tok2)
                    tok_qf = s_ev.inc(v_.scalar_tensor_tensor(qf[:, h, :], FB[2][:], fmB["fq2b"][:],
                                                              s1b[:], AluOpType.add, AluOpType.mult))
                    t_.wait_ge(s_ev.h, tok_qf)
                if sl == 0:
                    y_.wait_ge(s_cc.h, s_cc.n)
                    v_.wait_ge(s_ev.h, tok_kv0)
                    for rr in range(4):
                        tok_l = s_kvr.inc(y_.dma_start(
                            out=kvrank[:],
                            in_=cc_out[rr * 2048:(rr + 1) * 2048, :].rearrange("(h p) e -> p h e", p=128)),
                            16)
                        v_.wait_ge(s_kvr.h, tok_l)
                        tok_acc = s_ev.inc(v_.scalar_tensor_tensor(KVf[:], kvrank[:], pm[:, rr:rr + 1],
                                                                   KVf[:], AluOpType.mult, AluOpType.add))
                        y_.wait_ge(s_ev.h, tok_acc)
                    c_.wait_ge(s_ev.h, s_ev.n)
                    tok_kvbcvt[-1] = s_sc.inc(c_.copy(KVb[:], KVf[:]))
                v_.wait_ge(s_sl.h, SLAB_LOADS)
                t_.wait_ge(s_sl.h, SLAB_LOADS)
                for ci in range(CPS):
                    cg = sl * CPS + ci
                    for r in range(2):
                        for b in range(2):
                            bank = FB[b]
                            key = (cg, r, b)
                            prev = (cg, r - 1, b) if r > 0 else (cg - 1, 1, b)
                            if prev in tok_sbfb:
                                t_.wait_ge(s_ev.h, tok_sbfb[prev])
                            for i4 in range(4):
                                h = r * 8 + b * 4 + i4
                                mm = t_.matmul(bank[:, i4 * 128:(i4 + 1) * 128],
                                               kf_sl[:, h, ci * 128:(ci + 1) * 128],
                                               qf[:, h, ci * 128:(ci + 1) * 128], start=True, stop=True)
                            tok_sTb[key] = s_pe.inc(mm)
                            v_.wait_ge(s_pe.h, tok_sTb[key])
                            if (cg - 1, r, b) in tok_ob:
                                v_.wait_ge(s_pe.h, tok_ob[(cg - 1, r, b)])
                            tok_sbfb[key] = s_ev.inc(v_.tensor_tensor(
                                sbf_all[:, r * 8 + b * 4:r * 8 + b * 4 + 4, :],
                                bank[:].rearrange("p (a b) -> p a b", a=4),
                                maskf[:].unsqueeze(1).broadcast_to((128, 4, 128)), AluOpType.mult))
                        for b in range(2):
                            ob = FB[2 + r * 2 + b]
                            okey = (cg, r, b)
                            oprev = (cg - 1, r, b)
                            if oprev in tok_o1ev:
                                t_.wait_ge(s_sc.h, tok_o1ev[oprev])
                            t_.wait_ge(s_ev.h, tok_sbfb[(cg, r, b)])
                            t_.wait_ge(s_sc.h, tok_kvbcvt[sl - 1])
                            for i4 in range(4):
                                h = r * 8 + b * 4 + i4
                                t_.matmul(ob[:, i4 * 128:(i4 + 1) * 128],
                                          qf[:, h, ci * 128:(ci + 1) * 128], KVb[:, h, :],
                                          start=True, stop=False)
                                for jci in range(ci):
                                    t_.matmul(ob[:, i4 * 128:(i4 + 1) * 128],
                                              qf[:, h, ci * 128:(ci + 1) * 128], kvn[jci][:, h, :],
                                              start=False, stop=False)
                                mm = t_.matmul(ob[:, i4 * 128:(i4 + 1) * 128],
                                               sbf_all[:, h, :], vtm_sl[:, h, ci, :],
                                               start=False, stop=True)
                            tok_ob[okey] = s_pe.inc(mm)
                    for r in range(2):
                        for b in range(2):
                            c_.wait_ge(s_pe.h, tok_ob[(cg, r, b)])
                            if cg > 0 and r == 0 and b == 0:
                                c_.wait_ge(s_ev.h, tok_rmul)
                            sl4 = slice(r * 8 + b * 4, r * 8 + b * 4 + 4)
                            tok_o1ev[(cg, r, b)] = s_sc.inc(c_.copy(o1_all[:, sl4, :],
                                                                    FB[2 + r * 2 + b][:]))
                    v_.wait_ge(s_sc.h, tok_o1ev[(cg, 1, 1)])
                    if cg == 0:
                        v_.wait_ge(s_sc.h, tok_kvbcvt[-1])
                    for hh2 in range(2):
                        sl8 = slice(hh2 * 8, hh2 * 8 + 8)
                        v_.tensor_tensor(kvrank[:, sl8, :], o1_all[:, sl8, :],
                                         o1_all[:, sl8, :], AluOpType.mult)
                        v_.reduce_sum(ssc16[:, sl8], kvrank[:, sl8, :], axis=AX)
                        v_.tensor_scalar(ssc16[:, sl8], ssc16[:, sl8], 1.0 / DK, EPS,
                                         AluOpType.mult, AluOpType.add)
                        tok_var = s_ev.inc(v_.reciprocal(sd16[:, sl8], ssc16[:, sl8]))
                        c_.wait_ge(s_ev.h, tok_var)
                        tok_rs = s_sc.inc(c_.sqrt(rst16[:, sl8], sd16[:, sl8]))
                        v_.wait_ge(s_sc.h, tok_rs)
                        if cg >= 1 and hh2 == 0:
                            v_.wait_ge(s_pe.h, tok_trb[(cg - 1, 1)])
                        v_.tensor_tensor(kvrank[:, sl8, :], o1_all[:, sl8, :],
                                         rst16[:, sl8].unsqueeze(2).broadcast_to((128, 8, 128)),
                                         AluOpType.mult)
                        tok_o2[(cg, hh2)] = s_ev.inc(v_.tensor_tensor(
                            o2_all[:, sl8, :], kvrank[:, sl8, :],
                            nw_bc[:].unsqueeze(1).broadcast_to((128, 8, 128)), AluOpType.mult))
                    tok_rmul = tok_o2[(cg, 1)]
                    for tb in range(2):
                        t_.wait_ge(s_ev.h, tok_o2[(cg, tb)])
                        if (cg - 1, tb) in tok_oslev:
                            t_.wait_ge(s_sc.h, tok_oslev[(cg - 1, tb)])
                        for i8 in range(8):
                            h = tb * 8 + i8
                            mm = t_.transpose(TRB[tb][:, i8 * 128:(i8 + 1) * 128], o2_all[:, h, :], idb[:])
                        tok_trb[(cg, tb)] = s_pe.inc(mm)
                        c_.wait_ge(s_pe.h, tok_trb[(cg, tb)])
                        tok_oslev[(cg, tb)] = s_sc.inc(c_.copy(
                            o_sl[:, ci, tb * 8:(tb + 1) * 8, :], TRB[tb][:]))
                g_.wait_ge(s_pe.h, tok_ob[(sl * CPS + CPS - 1, 1, 1)])
                for jci in range(CPS - 1):
                    if sl == 0 and jci == 0:
                        g_.wait_ge(s_sc.h, tok_kvbcvt[-1])
                    else:
                        g_.wait_ge(s_gv.h, s_gv.n)
                    s_gv.inc(g_.tensor_tensor(KVf[:], KVf[:], kvn[jci][:], AluOpType.add))
                y_.wait_ge(s_gv.h, s_gv.n)
                tok_k3 = s_kvl[0].inc(y_.dma_start(
                    out=kvn[0][:], in_=kv_scr[:, sl * CPS + CPS - 1, :, :].rearrange("h p e -> p h e")), 16)
                g_.wait_ge(s_kvl[0].h, tok_k3)
                g_.wait_ge(s_gv.h, s_gv.n)
                s_gv.inc(g_.tensor_tensor(KVf[:], KVf[:], kvn[0][:], AluOpType.add))
                tok_baseupd[sl] = s_gv.n
                c_.wait_ge(s_gv.h, tok_baseupd[sl])
                tok_kvbcvt[sl] = s_sc.inc(c_.copy(KVb[:], KVf[:]))
                last_osl = s_sc.n
                for m in range(DC):
                    par = m % 2
                    gi = sl * DC + m
                    wb_bank = FB[2 + par]
                    t_.wait_ge(s_wo[par].h, tok_wo[gi])
                    t_.wait_ge(s_sc.h, last_osl)
                    if gi >= 2:
                        t_.wait_ge(s_ev.h, tok_Ae[gi - 2])
                    for hh in range(H):
                        mm = t_.matmul(wb_bank[:], wob[par][:, hh, :], o_sl[:, :, hh, :],
                                       start=(hh == 0), stop=(hh == H - 1))
                    tok_A = s_pe.inc(mm)
                    gi2 = gi + 2
                    if gi2 < SLABS * DC:
                        g_.wait_ge(s_pe.h, tok_A)
                        tok_wo[gi2] = load_wo(gi2, par)
                    v_.wait_ge(s_pe.h, tok_A)
                    if gi >= 1:
                        v_.wait_ge(s_A[(gi - 1) % 2].h, tok_Ast[gi - 1])
                    tok_Ae[gi] = s_ev.inc(v_.tensor_copy(A_sl[par][:], wb_bank[:]))
                    t_.wait_ge(s_ev.h, tok_Ae[gi])
                    y_.wait_ge(s_ev.h, tok_Ae[gi])
                    tok_Ast[gi] = s_A[par].inc(y_.dma_start(
                        out=A_scr[m * 128:(m + 1) * 128, sl * T_SLAB:(sl + 1) * T_SLAB],
                        in_=A_sl[par][:]), 16)

        with ExitStack() as ph:
            pes = ph.enter_context
            p_all = pes(nc.sbuf_tensor("p_all", [128, 16, 32], F32))
            c_all = pes(nc.sbuf_tensor("c_all", [128, 16, 32], F32))
            mk_all = pes(nc.sbuf_tensor("mk_all", [128, 16, 32], F32))
            w_all = pes(nc.sbuf_tensor("w_all", [128, 16, 32], F32))
            m0a = pes(nc.sbuf_tensor("m0a", [128, 16], F32))
            mia = pes(nc.sbuf_tensor("mia", [128, 16], F32))
            s8a = pes(nc.sbuf_tensor("s8a", [128, 16], F32))
            rca = pes(nc.sbuf_tensor("rca", [128, 16], F32))
            Wexp = pes(nc.sbuf_tensor("Wexp", [32, T], F32))
            ps_rl = pes(nc.psum_tensor("ps_rl", [128, 512], F32))
            ps_rt = pes(nc.psum_tensor("ps_rt", [128, 512], F32))

            for eng in (t_, nc.vector, nc.scalar):
                eng.wait_ge(s_ev.h, s_ev.n)
                eng.wait_ge(s_sc.h, s_sc.n)
                eng.wait_ge(s_pe.h, s_pe.n)
            for tt16 in range(16):
                for dc in range(DC):
                    mm = t_.matmul(ps_rl[:, tt16 * 32:(tt16 + 1) * 32],
                                   xn_fm[:, dc, tt16 * 128:(tt16 + 1) * 128], rT[:, dc, :],
                                   start=(dc == 0), stop=(dc == DC - 1))
            tok_logits = s_pe.inc(mm)
            v_.wait_ge(s_pe.h, tok_logits)
            v_.reduce_max(m0a[:], ps_rl[:].rearrange("p (a b) -> p a b", a=16), axis=AX)
            v_.tensor_tensor(p_all[:], ps_rl[:].rearrange("p (a b) -> p a b", a=16),
                             m0a[:].unsqueeze(2).broadcast_to((128, 16, 32)), AluOpType.subtract)
            tok_sub = s_ev.inc(v_.memset(s8a[:], 0.0))
            c_.wait_ge(s_ev.h, tok_sub)
            tok_exp = s_sc.inc(c_.activation(p_all[:], p_all[:], AF.Exp))
            v_.wait_ge(s_sc.h, tok_exp)
            v_.tensor_copy(c_all[:], p_all[:])
            for it in range(K_TOP):
                v_.reduce_max(mia[:], c_all[:], axis=AX)
                v_.tensor_tensor(s8a[:], s8a[:], mia[:], AluOpType.add)
                if it < K_TOP - 1:
                    v_.tensor_tensor(mk_all[:], c_all[:],
                                     mia[:].unsqueeze(2).broadcast_to((128, 16, 32)), AluOpType.is_equal)
                    v_.scalar_tensor_tensor(c_all[:], mk_all[:], -1e30, c_all[:],
                                            AluOpType.mult, AluOpType.add)
            v_.tensor_tensor(mk_all[:], p_all[:],
                             mia[:].unsqueeze(2).broadcast_to((128, 16, 32)), AluOpType.is_ge)
            v_.reciprocal(rca[:], s8a[:])
            v_.tensor_tensor(w_all[:], p_all[:], mk_all[:], AluOpType.mult)
            tok_w = s_ev.inc(v_.tensor_tensor(w_all[:], w_all[:],
                                              rca[:].unsqueeze(2).broadcast_to((128, 16, 32)),
                                              AluOpType.mult))
            t_.wait_ge(s_ev.h, tok_w)
            tok_wev = {}
            for tt16 in range(16):
                trbank = ps_rt if tt16 % 2 == 0 else ps_rl
                if tt16 >= 2:
                    t_.wait_ge(s_ev.h, tok_wev[tt16 - 2])
                tok_trr = s_pe.inc(t_.transpose(trbank[0:32, 0:128], w_all[:, tt16, :], idf[:]))
                v_.wait_ge(s_pe.h, tok_trr)
                tok_wev[tt16] = s_ev.inc(v_.tensor_copy(
                    Wexp[:, tt16 * 128:(tt16 + 1) * 128], trbank[0:32, 0:128]))
            y_.wait_ge(s_ev.h, tok_wev[15])
            s_wsc.inc(y_.dma_start(out=w_scr, in_=Wexp[:]), 16)
            y_.wait_ge(s_wsc.h, s_wsc.n)

    with ExitStack() as ph:
        pes = ph.enter_context
        xnh = pes(nc.sbuf_tensor("xnh", [128, DC, T_HALF], BF16))
        acc = pes(nc.sbuf_tensor("acc_s", [128, DC, T_HALF], F32))
        wgs = [pes(nc.sbuf_tensor(f"wg{s}", [128, DC, I_BLK], BF16)) for s in range(2)]
        wus = [pes(nc.sbuf_tensor(f"wu{s}", [128, DC, I_BLK], BF16)) for s in range(2)]
        wds = pes(nc.sbuf_tensor("wd_s", [128, 6, D], BF16))
        hbf = pes(nc.sbuf_tensor("h_s", [128, 6, T_HALF], BF16))
        sg = [pes(nc.sbuf_tensor(f"sg{i}", [128, 512], F32)) for i in range(2)]
        t1 = [pes(nc.sbuf_tensor(f"t1{i}", [128, 512], F32)) for i in range(2)]
        wrow = [pes(nc.sbuf_tensor(f"wrow{i}", [1, 512], F32)) for i in range(2)]
        Abuf = [pes(nc.sbuf_tensor(f"Abuf{i}", [128, T_HALF], F32)) for i in range(2)]
        outt = [pes(nc.sbuf_tensor(f"outt{i}", [128, D], F32)) for i in range(2)]
        ps_g = [pes(nc.psum_tensor(f"ps_g{i}", [128, 512], F32)) for i in range(2)]
        ps_u = [pes(nc.psum_tensor(f"ps_u{i}", [128, 512], F32)) for i in range(2)]
        ps_d = [pes(nc.psum_tensor(f"ps_d{i}", [128, 512], F32)) for i in range(2)]
        ps_w = [pes(nc.psum_tensor(f"ps_w{i}", [128, 512], F32)) for i in range(2)]

        for eng in (g_, y_, nc.vector, nc.scalar, nc.tensor):
            eng.wait_ge(s_A[0].h, s_A[0].n)
            eng.wait_ge(s_A[1].h, s_A[1].n)
            eng.wait_ge(s_ev.h, s_ev.n)
            eng.wait_ge(s_sc.h, s_sc.n)
            eng.wait_ge(s_pe.h, s_pe.n)
            eng.wait_ge(s_wsc.h, 16)

        def load_gu(j, half):
            e, ib = (j // IB) % NE, j % IB
            slot = j % 2
            sem = s_gu[slot]
            src_g = wg_in[e, :, ib * I_BLK:(ib + 1) * I_BLK].rearrange("(dc p) f -> p dc f", p=128)
            src_u = wu_in[e, :, ib * I_BLK:(ib + 1) * I_BLK].rearrange("(dc p) f -> p dc f", p=128)
            sem.inc(g_.dma_start(out=wgs[slot][:], in_=src_g), 16)
            return sem.inc(g_.dma_start(out=wus[slot][:], in_=src_u), 16)

        def load_wd(e):
            src = wd_in[e].rearrange("(ic p) f -> p ic f", p=128)
            return s_wd.inc(g_.dma_start(out=wds[:], in_=src), 16)

        tok_gu_loaded = {}
        tok_wd_loaded = {}
        tok_sg = {}
        tok_u_free = {}
        tok_w_free = {}
        tok_wb = {}
        tok_h = {}
        tok_dacc = {}
        tok_wrow = {}
        step = 0
        dstep = 0
        tok_gu_loaded[0] = load_gu(0, 0)
        tok_gu_loaded[1] = load_gu(1, 0)
        tok_wd_loaded[0] = load_wd(0)

        for half in range(2):
            h0 = half * T_HALF
            if half > 0:
                y_.wait_ge(s_pe.h, s_pe.n)
            tok_xnf = s_xnf.n
            for dc in range(DC):
                s_xnf.inc(y_.dma_start(out=xnh[:, dc, :],
                                       in_=xn_scr[h0:h0 + T_HALF, dc * 128:(dc + 1) * 128],
                                       transpose=True), 16)
            XNF = s_xnf.n
            if half > 0:
                v_.wait_ge(s_pe.h, s_pe.n)

            y_.wait_ge(s_wsc.h, 16)
            t_.wait_ge(s_xnf.h, XNF)
            t_.wait_ge(s_ev.h, tok_ones)
            if half > 0:
                y_.wait_ge(s_pe.h, s_pe.n)
            for tsw in range(TSF):
                tok_wrow[(half, 0, tsw)] = s_wr[tsw].inc(y_.dma_start(
                    out=wrow[tsw][:], in_=w_scr[0:1, h0 + tsw * 512:h0 + (tsw + 1) * 512]), 16)

            for e in range(NE):
                for ib in range(IB):
                    j = half * IB * NE + IB * e + ib
                    t_.wait_ge(s_gu[j % 2].h, tok_gu_loaded[j])
                    for ts in range(TSF):
                        if ib == 0:
                            if tok_w_free.get(ts):
                                t_.wait_ge(s_ev.h, tok_w_free[ts])
                            t_.wait_ge(s_wr[ts].h, tok_wrow[(half, e, ts)])
                            mmw = t_.matmul(ps_w[ts][:], ones_row[:], wrow[ts][:],
                                            start=True, stop=True)
                            tok_wb[ts] = s_pe.inc(mmw)
                            if e + 1 < NE:
                                y_.wait_ge(s_pe.h, tok_wb[ts])
                                tok_wrow[(half, e + 1, ts)] = s_wr[ts].inc(
                                    y_.dma_start(out=wrow[ts][:],
                                                 in_=w_scr[e + 1:e + 2, h0 + ts * 512:h0 + (ts + 1) * 512]),
                                    16)
                        for ic in range(ICB):
                            par = step % 2
                            icg = ib * ICB + ic
                            if step >= 2:
                                t_.wait_ge(s_ev.h, tok_u_free[step - 2])
                            for dc in range(DC):
                                mm = t_.matmul(ps_g[par][:], wgs[j % 2][:, dc, ic * 128:(ic + 1) * 128],
                                               xnh[:, dc, ts * 512:(ts + 1) * 512],
                                               start=(dc == 0), stop=(dc == DC - 1))
                            tok_gmm = s_pe.inc(mm)
                            for dc in range(DC):
                                mm = t_.matmul(ps_u[par][:], wus[j % 2][:, dc, ic * 128:(ic + 1) * 128],
                                               xnh[:, dc, ts * 512:(ts + 1) * 512],
                                               start=(dc == 0), stop=(dc == DC - 1))
                            tok_umm = s_pe.inc(mm)
                            c_.wait_ge(s_pe.h, tok_gmm)
                            if step >= 2:
                                c_.wait_ge(s_ev.h, tok_u_free[step - 2])
                            tok_sg[step] = s_sc.inc(c_.activation(sg[par][:], ps_g[par][:], AF.Silu))
                            v_.wait_ge(s_sc.h, tok_sg[step])
                            v_.wait_ge(s_pe.h, tok_umm)
                            v_.wait_ge(s_pe.h, tok_wb[ts])
                            v_.tensor_tensor(t1[par][:], sg[par][:], ps_u[par][:], AluOpType.mult)
                            tok_u_free[step] = s_ev.inc(
                                v_.tensor_tensor(hbf[:, icg, ts * 512:(ts + 1) * 512], t1[par][:],
                                                 ps_w[ts][:], AluOpType.mult))
                            if (ib, ic) == (1, ICB - 1):
                                tok_h[ts] = tok_u_free[step]
                                tok_w_free[ts] = tok_u_free[step]
                            step += 1
                    tok_gu_done = s_pe.n
                    if j + 2 < 2 * IB * NE:
                        g_.wait_ge(s_pe.h, tok_gu_done)
                        tok_gu_loaded[j + 2] = load_gu(j + 2, half)
                ge = half * NE + e
                t_.wait_ge(s_wd.h, tok_wd_loaded[ge])
                for ts in range(TSF):
                    t_.wait_ge(s_ev.h, tok_h[ts])
                    for m in range(DC):
                        par = dstep % 2
                        if dstep >= 2:
                            t_.wait_ge(s_ev.h, tok_dacc[dstep - 2])
                        for ic in range(6):
                            mm = t_.matmul(ps_d[par][:], wds[:, ic, m * 128:(m + 1) * 128],
                                           hbf[:, ic, ts * 512:(ts + 1) * 512],
                                           start=(ic == 0), stop=(ic == 5))
                        tok_dm = s_pe.inc(mm)
                        v_.wait_ge(s_pe.h, tok_dm)
                        if e == 0:
                            tok_dacc[dstep] = s_ev.inc(
                                v_.tensor_copy(acc[:, m, ts * 512:(ts + 1) * 512], ps_d[par][:]))
                        else:
                            tok_dacc[dstep] = s_ev.inc(
                                v_.tensor_tensor(acc[:, m, ts * 512:(ts + 1) * 512],
                                                 acc[:, m, ts * 512:(ts + 1) * 512],
                                                 ps_d[par][:], AluOpType.add))
                        dstep += 1
                tok_d_done = tok_dacc[dstep - 1]
                if ge + 1 < 2 * NE:
                    g_.wait_ge(s_pe.h, s_pe.n)
                    tok_wd_loaded[ge + 1] = load_wd((ge + 1) % NE)

            tok_Al = {}
            for mi in range(2):
                tok_Al[mi] = s_Al[mi].inc(y_.dma_start(
                    out=Abuf[mi][:], in_=A_scr[mi * 128:(mi + 1) * 128, h0:h0 + T_HALF]), 16)
            tok_accA = {}
            for m in range(DC):
                apar = m % 2
                v_.wait_ge(s_Al[apar].h, tok_Al[m])
                v_.wait_ge(s_ev.h, tok_d_done)
                tok_accA[m] = s_ev.inc(v_.tensor_tensor(acc[:, m, :], acc[:, m, :], Abuf[apar][:],
                                                        AluOpType.add))
                if m + 2 < DC:
                    y_.wait_ge(s_ev.h, tok_accA[m])
                    tok_Al[m + 2] = s_Al[apar].inc(y_.dma_start(
                        out=Abuf[apar][:], in_=A_scr[(m + 2) * 128:(m + 3) * 128, h0:h0 + T_HALF]), 16)
            ACCA = s_ev.n
            y_.wait_ge(s_out[0].h, s_out[0].n)
            y_.wait_ge(s_out[1].h, s_out[1].n)
            tok_xo = {0: s_xo[0].inc(y_.dma_start(out=outt[0][:], in_=x_in[h0:h0 + 128, :]), 16),
                      1: s_xo[1].inc(y_.dma_start(out=outt[1][:], in_=x_in[h0 + 128:h0 + 256, :]), 16)}
            tok_outst = {}
            tok_outadd = {}
            for ttile in range(8):
                opar = ttile % 2
                t_.wait_ge(s_ev.h, ACCA)
                for mg in range(4):
                    bank = ps_d[mg % 2]
                    gidx = ttile * 4 + mg
                    if gidx >= 2:
                        t_.wait_ge(s_ev.h, tok_outadd[gidx - 2])
                    for mi4 in range(4):
                        m = mg * 4 + mi4
                        mm = t_.transpose(bank[:, mi4 * 128:(mi4 + 1) * 128],
                                          acc[:, m, ttile * 128:(ttile + 1) * 128], idf[:])
                    tok_trm = s_pe.inc(mm)
                    v_.wait_ge(s_pe.h, tok_trm)
                    if mg == 0:
                        v_.wait_ge(s_xo[opar].h, tok_xo[ttile])
                    tok_outadd[gidx] = s_ev.inc(
                        v_.tensor_tensor(outt[opar][:, mg * 512:(mg + 1) * 512],
                                         outt[opar][:, mg * 512:(mg + 1) * 512], bank[:],
                                         AluOpType.add))
                y_.wait_ge(s_ev.h, tok_outadd[ttile * 4 + 3])
                tok_outst[ttile] = s_out[opar].inc(y_.dma_start(
                    out=out_dram[h0 + ttile * 128:h0 + (ttile + 1) * 128, :], in_=outt[opar][:]), 16)
                if ttile + 2 < 8:
                    y_.wait_ge(s_out[opar].h, tok_outst[ttile])
                    tok_xo[ttile + 2] = s_xo[opar].inc(y_.dma_start(
                        out=outt[opar][:], in_=x_in[h0 + (ttile + 2) * 128:h0 + (ttile + 3) * 128, :]), 16)
        y_.wait_ge(s_out[0].h, s_out[0].n)
        y_.wait_ge(s_out[1].h, s_out[1].n)

    stack.close()
    return nc


_NC_CACHE = None


def kernel(L0_attn_x, L0_attn_wq, L0_attn_wk, L0_attn_wv, L0_attn_wo,
           L0_attn_fq1_w, L0_attn_fq1_b, L0_attn_fq2_w, L0_attn_fq2_b,
           L0_attn_fk1_w, L0_attn_fk1_b, L0_attn_fk2_w, L0_attn_fk2_b,
           L0_attn_norm_w, L0_ffn_router_w, L0_ffn_w_gate, L0_ffn_w_up, L0_ffn_w_down):
    global _NC_CACHE
    f32 = np.float32
    A = lambda a: np.ascontiguousarray(np.asarray(a, dtype=f32))
    x = A(L0_attn_x)
    common = {
        "wqT": A(np.asarray(L0_attn_wq).T), "wkT": A(np.asarray(L0_attn_wk).T),
        "wvT": A(np.asarray(L0_attn_wv).T), "woT": A(np.asarray(L0_attn_wo).T),
        "fq1T": A(np.asarray(L0_attn_fq1_w).T), "fq2T": A(np.asarray(L0_attn_fq2_w).T),
        "fk1T": A(np.asarray(L0_attn_fk1_w).T), "fk2T": A(np.asarray(L0_attn_fk2_w).T),
        "fq1b": A(L0_attn_fq1_b).reshape(128, 1), "fq2b": A(L0_attn_fq2_b).reshape(128, 1),
        "fk1b": A(L0_attn_fk1_b).reshape(128, 1), "fk2b": A(L0_attn_fk2_b).reshape(128, 1),
        "nw_bc": A(np.broadcast_to(np.asarray(L0_attn_norm_w, dtype=f32).reshape(1, 128), (128, 128))),
        "maskf": np.triu(np.ones((128, 128), f32)),
        "id_bf": np.eye(128, dtype=ml_dtypes.bfloat16),
        "id_f32": np.eye(128, dtype=f32),
        "routerT": A(np.asarray(L0_ffn_router_w).T),
        "wg": A(L0_ffn_w_gate), "wu": A(L0_ffn_w_up), "wd": A(L0_ffn_w_down),
    }
    in_maps = []
    for core in range(8):
        b, qt = core // 4, core % 4
        pmv = np.zeros((128, 4), f32)
        pmv[:, :qt] = 1.0
        in_maps.append({**common, "x": np.ascontiguousarray(x[b, qt * T:(qt + 1) * T]), "pm": pmv})

    if _NC_CACHE is None:
        _NC_CACHE = build()
    res = run_bass_kernel_spmd(_NC_CACHE, in_maps, core_ids=list(range(8)))
    out = np.empty((2, 8192, D), f32)
    for core in range(8):
        b, qt = core // 4, core % 4
        out[b, qt * T:(qt + 1) * T] = res.results[core]["out"]
    return out
